# version 01
# baseline (speedup 1.0000x reference)
import numpy as np
import concourse.bass as bass
import concourse.mybir as mybir

F32 = mybir.dt.float32
BF16 = mybir.dt.bfloat16
I32 = mybir.dt.int32
U8 = mybir.dt.uint8
AF = mybir.ActivationFunctionType
ALU = mybir.AluOpType
AX = mybir.AxisListType

ENGINES = ("tensor", "vector", "scalar", "gpsimd", "sync")
QUEUES = ("qsp", "qact", "qpool")


class Task:
    __slots__ = ("eng", "count")

    def __init__(self, eng, count):
        self.eng = eng
        self.count = count


class Ctx:

    def __init__(self, nc, es):
        self.nc = nc
        self.sem = {}
        self.count = {e: 0 for e in ENGINES + QUEUES}
        for e in ENGINES + QUEUES:
            self.sem[e] = es.enter_context(nc.semaphore(f"sem_{e}"))
        self.cur_eng = None

    def engine(self, name):
        return getattr(self.nc, name)

    def _issuer(self, q):
        return {"qsp": "sync", "qact": "scalar", "qpool": "gpsimd"}[q]

    class _TaskGuard:
        def __init__(self, ctx, eng, deps, barrier_ok=False):
            self.ctx, self.eng, self.deps = ctx, eng, deps
            self.last = None
            self.barrier_ok = barrier_ok

        def __enter__(self):
            c, eng = self.ctx, self.eng
            h = c.engine(eng)
            waits = {}
            for d in self.deps:
                if d is None or d.eng == eng:
                    continue
                waits[d.eng] = max(waits.get(d.eng, 0), d.count)
            for de, val in sorted(waits.items()):
                h.wait_ge(c.sem[de], val)
            return self

        def done(self, inst):
            self.last = inst
            return inst

        def __exit__(self, et, ev, tb):
            if et is not None:
                return False
            c, eng = self.ctx, self.eng
            h = c.engine(eng)
            c.count[eng] += 1
            if self.last is not None:
                self.last.then_inc(c.sem[eng], 1)
            else:
                assert self.barrier_ok, (
                    f"task on {eng} ended without done(last_instruction); "
                    "nop-inc is only safe for pure barrier tasks")
                h.nop().then_inc(c.sem[eng], 1)
            self.token = Task(eng, c.count[eng])
            return False

    def task(self, eng, deps=()):
        return Ctx._TaskGuard(self, eng, list(deps))

    def btask(self, eng, deps=()):
        return Ctx._TaskGuard(self, eng, list(deps), barrier_ok=True)

    def dma(self, queue, out, in_, deps=(), **kw):
        eng = self._issuer(queue)
        h = self.engine(eng)
        waits = {}
        for d in deps:
            if d is None:
                continue
            waits[d.eng] = max(waits.get(d.eng, 0), d.count)
        for de, val in sorted(waits.items()):
            h.wait_ge(self.sem[de], val)
        h.dma_start(out=out, in_=in_, **kw).then_inc(self.sem[queue], 16)
        self.count[queue] += 16
        return Task(queue, self.count[queue])

    def barrier_task(self, eng, deps):
        with self.btask(eng, deps) as t:
            pass
        return t.token

    def wait_all_dmas(self, eng):
        h = self.engine(eng)
        for q in QUEUES:
            if self.count[q]:
                h.wait_ge(self.sem[q], self.count[q])


class Buf:

    __slots__ = ("writer", "readers")

    def __init__(self):
        self.writer = None
        self.readers = []

    def rdeps(self):
        return [self.writer] if self.writer is not None else []

    def wdeps(self):
        d = list(self.readers)
        if self.writer is not None:
            d.append(self.writer)
        return d

    def wrote(self, task):
        self.writer = task
        self.readers = []

    def read(self, task):
        self.readers.append(task)


def ceil_div(a, b):
    return -(-a // b)


CH = 1024
NG = 8
C = 8
TOK = 4096
NCH = TOK // C
SLAB = 2048
NSLAB = TOK // SLAB
CPS = SLAB // C


def build_ph_consts(ctx, nc, es, phz_d, pin_d, nlev):
    v = nc.vector
    sc = nc.scalar
    G = []
    lds = []
    for g in range(NG):
        phz = es.enter_context(nc.sbuf_tensor(f"phz{g}", [128, 2], F32))
        pin = es.enter_context(nc.sbuf_tensor(f"pin{g}", [128, 2], F32))
        lds.append(ctx.dma("qsp", phz[:], phz_d[128 * g:128 * (g + 1), :]))
        lds.append(ctx.dma("qsp", pin[:], pin_d[128 * g:128 * (g + 1), :]))
        w = es.enter_context(nc.sbuf_tensor(f"phw{g}", [128, 8], F32))
        p0re = es.enter_context(nc.sbuf_tensor(f"p0re{g}", [128, C + 1], F32))
        p0im = es.enter_context(nc.sbuf_tensor(f"p0im{g}", [128, C + 1], F32))
        mt = es.enter_context(nc.sbuf_tensor(f"mt{g}", [128, 2 * (nlev + 1)], F32))
        G.append(dict(phz=phz, pin=pin, w=w, p0re=p0re, p0im=p0im, mt=mt))
    with ctx.task("scalar", lds) as t:
        for g in range(NG):
            sc.activation(G[g]["w"][:, 0:1], G[g]["phz"][:, 0:1], AF.Square)
        for g in range(NG):
            last = sc.activation(G[g]["w"][:, 1:2], G[g]["phz"][:, 1:2], AF.Square)
        t.done(last)
    with ctx.task("vector", [t.token]) as t:
        for g in range(NG):
            last = v.tensor_add(G[g]["w"][:, 2:3], G[g]["w"][:, 0:1],
                                G[g]["w"][:, 1:2])
        t.done(last)
    with ctx.task("scalar", [t.token]) as t:
        for g in range(NG):
            sc.activation(G[g]["w"][:, 3:4], G[g]["w"][:, 2:3], AF.Sqrt)
        for g in range(NG):
            last = sc.activation(G[g]["w"][:, 4:5], G[g]["w"][:, 3:4], AF.Exp,
                                 scale=-1.0)
        t.done(last)
    with ctx.task("vector", [t.token]) as t:
        for g in range(NG):
            v.reciprocal(G[g]["w"][:, 5:6], G[g]["w"][:, 3:4])
        for g in range(NG):
            v.tensor_mul(G[g]["w"][:, 6:7], G[g]["w"][:, 4:5], G[g]["w"][:, 5:6])
        for g in range(NG):
            v.tensor_mul(G[g]["w"][:, 0:1], G[g]["phz"][:, 0:1], G[g]["w"][:, 6:7])
        for g in range(NG):
            last = v.tensor_mul(G[g]["w"][:, 1:2], G[g]["phz"][:, 1:2],
                                G[g]["w"][:, 6:7])
        t.done(last)
    with ctx.task("vector", [t.token]) as t:
        for g in range(NG):
            v.memset(G[g]["p0re"][:, 0:1], 1.0)
            v.memset(G[g]["p0im"][:, 0:1], 0.0)
        for g in range(NG):
            v.tensor_copy(G[g]["p0re"][:, 1:2], G[g]["w"][:, 0:1])
            v.tensor_copy(G[g]["p0im"][:, 1:2], G[g]["w"][:, 1:2])
        for j in range(1, C):
            for g in range(NG):
                gg = G[g]
                a, b = gg["w"][:, 0:1], gg["w"][:, 1:2]
                pr, pi_ = gg["p0re"], gg["p0im"]
                v.tensor_scalar_mul(gg["w"][:, 2:3], pi_[:, j:j + 1], b)
                v.tensor_scalar_mul(gg["w"][:, 3:4], pr[:, j:j + 1], b)
                v.scalar_tensor_tensor(pr[:, j + 1:j + 2], pr[:, j:j + 1], a,
                                       gg["w"][:, 2:3], ALU.mult, ALU.subtract)
                last = v.scalar_tensor_tensor(pi_[:, j + 1:j + 2],
                                              pi_[:, j:j + 1], a,
                                              gg["w"][:, 3:4], ALU.mult, ALU.add)
        t.done(last)
    with ctx.task("vector", [t.token]) as t:
        for g in range(NG):
            v.tensor_copy(G[g]["mt"][:, 0:1], G[g]["p0re"][:, C:C + 1])
            v.tensor_copy(G[g]["mt"][:, 1:2], G[g]["p0im"][:, C:C + 1])
        for li in range(1, nlev + 1):
            for g in range(NG):
                mt, w = G[g]["mt"], G[g]["w"]
                pa = mt[:, 2 * li - 2:2 * li - 1]
                pb = mt[:, 2 * li - 1:2 * li]
                v.tensor_mul(w[:, 4:5], pa, pa)
                v.tensor_mul(w[:, 5:6], pb, pb)
                v.tensor_mul(w[:, 6:7], pa, pb)
                v.tensor_sub(mt[:, 2 * li:2 * li + 1], w[:, 4:5], w[:, 5:6])
                last = v.tensor_add(mt[:, 2 * li + 1:2 * li + 2], w[:, 6:7],
                                    w[:, 6:7])
        t.done(last)
    tg = t.token
    for g in range(NG):
        gg = G[g]
        gg["a"], gg["b"] = gg["w"][:, 0:1], gg["w"][:, 1:2]
        gg["pire"], gg["piim"] = gg["pin"][:, 0:1], gg["pin"][:, 1:2]
    return G, tg


def build(nc, es):
    ctx = Ctx(nc, es)
    y0 = nc.dram_tensor("y0", [TOK, CH], F32, kind="ExternalInput").ap()
    phz_d = nc.dram_tensor("phazor", [CH, 2], F32, kind="ExternalInput").ap()
    pin_d = nc.dram_tensor("phazor_init", [CH, 2], F32, kind="ExternalInput").ap()
    z1_d = nc.dram_tensor("z1", [CH, TOK], F32, kind="ExternalOutput").ap()
    clre_d = nc.dram_tensor("clre", [CH, NCH], F32, kind="ExternalOutput").ap()
    clim_d = nc.dram_tensor("clim", [CH, NCH], F32, kind="ExternalOutput").ap()
    ef_d = nc.dram_tensor("ef", [CH, 2], F32, kind="ExternalOutput").ap()

    v = nc.vector
    te = nc.tensor

    offs = []
    off = 0
    wdt = NCH
    while wdt > 1:
        wdt //= 2
        offs.append((off, wdt))
        off += wdt + 1
    nlev = len(offs)

    G, t_ph = build_ph_consts(ctx, nc, es, phz_d, pin_d, nlev)

    ident = es.enter_context(nc.sbuf_tensor("ident", [128, 128], F32))
    ii = es.enter_context(nc.sbuf_tensor("ii", [128, 128], I32))
    ci = es.enter_context(nc.sbuf_tensor("ci", [128, 128], I32))
    with ctx.task("gpsimd", []) as t:
        nc.gpsimd.iota(ci[:], pattern=[[1, 128]], base=0, channel_multiplier=0)
        t.done(nc.gpsimd.iota(ii[:], pattern=[[0, 128]], base=0,
                              channel_multiplier=1))
    with ctx.task("vector", [t.token]) as t:
        v.tensor_tensor(ii[:], ci[:], ii[:], ALU.is_equal)
        t.done(v.tensor_copy(ident[:], ii[:]))
    t_ident = t.token

    yfm = [es.enter_context(nc.sbuf_tensor(f"yfm{g}", [128, SLAB], F32))
           for g in range(NG)]
    ybuf = [Buf() for _ in range(NG)]
    ytm = [es.enter_context(nc.sbuf_tensor(f"ytm{i}", [128, CH], F32))
           for i in range(2)]
    ytm_buf = [Buf(), Buf()]
    ps = [es.enter_context(nc.psum_tensor(f"ps{i}", [128, 512], F32))
          for i in range(4)]
    ps_buf = [Buf() for _ in range(4)]
    sre = [es.enter_context(nc.sbuf_tensor(f"sre{g}", [128, CPS], F32))
           for g in range(NG)]
    sim = [es.enter_context(nc.sbuf_tensor(f"sim{g}", [128, CPS], F32))
           for g in range(NG)]
    tmp1 = [es.enter_context(nc.sbuf_tensor(f"tmp1{g}", [128, CPS], F32))
            for g in range(NG)]
    tmp2 = [es.enter_context(nc.sbuf_tensor(f"tmp2{g}", [128, CPS], F32))
            for g in range(NG)]
    ere = [es.enter_context(nc.sbuf_tensor(f"ere{g}", [128, 1 + NCH], F32))
           for g in range(NG)]
    eim = [es.enter_context(nc.sbuf_tensor(f"eim{g}", [128, 1 + NCH], F32))
           for g in range(NG)]
    crs = [es.enter_context(nc.sbuf_tensor(f"crs{g}", [128, NCH + 16], F32))
           for g in range(NG)]
    cis = [es.enter_context(nc.sbuf_tensor(f"cis{g}", [128, NCH + 16], F32))
           for g in range(NG)]

    out_tasks = []
    escan = [[] for _ in range(NG)]

    with ctx.task("vector", []) as t:
        for g in range(NG):
            v.memset(ere[g][:, 0:1], 0.0)
            last = v.memset(eim[g][:, 0:1], 0.0)
        t.done(last)
    t_einit = t.token

    for slab in range(NSLAB):
        tok0 = slab * SLAB
        trans_tasks = [[] for _ in range(NG)]
        for tt in range(SLAB // 128):
            bi = tt % 2
            tile, tbuf = ytm[bi], ytm_buf[bi]
            tld = ctx.dma("qsp", tile[:],
                          y0[tok0 + 128 * tt: tok0 + 128 * (tt + 1), :],
                          deps=tbuf.wdeps())
            tbuf.wrote(tld)
            for g in range(NG):
                pst, psb = ps[g % 4], ps_buf[g % 4]
                with ctx.task("tensor", [tld, t_ident] + psb.wdeps()) as t:
                    t.done(te.transpose(pst[:, 0:128],
                                        tile[:, 128 * g:128 * (g + 1)],
                                        ident[:]))
                psb.wrote(t.token)
                tbuf.read(t.token)
                d2 = [t.token] + (ybuf[g].wdeps() if tt == 0 else [])
                with ctx.task("vector", d2) as t2:
                    t2.done(v.tensor_copy(yfm[g][:, 128 * tt:128 * (tt + 1)],
                                          pst[:, 0:128]))
                psb.read(t2.token)
                trans_tasks[g].append(t2.token)

        for g in range(NG):
            gg = G[g]
            a, b = gg["a"], gg["b"]
            pire, piim = gg["pire"], gg["piim"]
            yv = yfm[g][:].rearrange("p (c j) -> p j c", j=C)
            with ctx.task("vector", trans_tasks[g] + [t_ph, t_einit]) as t:
                for j in range(C):
                    yj = yv[:, j, :]
                    if j == 0:
                        v.tensor_copy(sre[g][:], yj)
                        v.memset(sim[g][:], 0.0)
                    else:
                        v.scalar_tensor_tensor(tmp1[g][:], sim[g][:], b, yj,
                                               ALU.mult, ALU.subtract)
                        v.tensor_scalar_mul(tmp2[g][:], sre[g][:], b)
                        v.scalar_tensor_tensor(sre[g][:], sre[g][:], a,
                                               tmp1[g][:], ALU.mult,
                                               ALU.subtract)
                        v.scalar_tensor_tensor(sim[g][:], sim[g][:], a,
                                               tmp2[g][:], ALU.mult, ALU.add)
                    v.scalar_tensor_tensor(tmp1[g][:], sim[g][:], piim, yj,
                                           ALU.mult, ALU.subtract)
                    v.scalar_tensor_tensor(yj, sre[g][:], pire, tmp1[g][:],
                                           ALU.mult, ALU.subtract)
                cb = 1 + slab * CPS
                v.tensor_copy(ere[g][:, cb:cb + CPS], sre[g][:])
                t.done(v.tensor_copy(eim[g][:, cb:cb + CPS], sim[g][:]))
            ybuf[g].wrote(t.token)
            escan[g].append(t.token)

        for g in range(NG):
            tst = ctx.dma("qact", z1_d[128 * g:128 * (g + 1), tok0:tok0 + SLAB],
                          yfm[g][:], deps=ybuf[g].rdeps())
            ybuf[g].read(tst)
            out_tasks.append(tst)

    dep = [t_ph, t_einit] + [tk for g in range(NG) for tk in escan[g]]
    lv = [(ere[g], eim[g], 1, NCH) for g in range(NG)]
    parents = [[] for _ in range(NG)]
    for li, (o, wd) in enumerate(offs):
        with ctx.task("vector", dep) as t:
            for g in range(NG):
                v.memset(crs[g][:, o:o + 1], 0.0)
                v.memset(cis[g][:, o:o + 1], 0.0)
            for g in range(NG):
                lre, lim, in_off, in_w = lv[g]
                dat_re = lre[:, in_off:in_off + in_w].rearrange(
                    "p (u two) -> p two u", two=2)
                dat_im = lim[:, in_off:in_off + in_w].rearrange(
                    "p (u two) -> p two u", two=2)
                ev_re, od_re = dat_re[:, 0, :], dat_re[:, 1, :]
                ev_im, od_im = dat_im[:, 0, :], dat_im[:, 1, :]
                ma = G[g]["mt"][:, 2 * li:2 * li + 1]
                mb = G[g]["mt"][:, 2 * li + 1:2 * li + 2]
                v.scalar_tensor_tensor(tmp1[g][:, :wd], ev_im, mb, od_re,
                                       ALU.mult, ALU.subtract)
                v.scalar_tensor_tensor(tmp2[g][:, :wd], ev_re, mb, od_im,
                                       ALU.mult, ALU.add)
                v.scalar_tensor_tensor(crs[g][:, o + 1:o + 1 + wd], ev_re, ma,
                                       tmp1[g][:, :wd], ALU.mult, ALU.subtract)
                last = v.scalar_tensor_tensor(cis[g][:, o + 1:o + 1 + wd],
                                              ev_im, ma, tmp2[g][:, :wd],
                                              ALU.mult, ALU.add)
                parents[g].append(lv[g])
                lv[g] = (crs[g], cis[g], o + 1, wd)
            t.done(last)
        dep = [t.token]

    for li in range(nlev - 1, -1, -1):
        o, wd = offs[li]
        with ctx.task("vector", dep) as t:
            for g in range(NG):
                p_re, p_im, p_off, p_w = parents[g][li]
                ch_re = crs[g][:, o:o + wd + 1]
                ch_im = cis[g][:, o:o + wd + 1]
                pdat_re = p_re[:, p_off:p_off + p_w].rearrange(
                    "p (u two) -> p two u", two=2)
                pdat_im = p_im[:, p_off:p_off + p_w].rearrange(
                    "p (u two) -> p two u", two=2)
                pev_re, pod_re = pdat_re[:, 0, :], pdat_re[:, 1, :]
                pev_im, pod_im = pdat_im[:, 0, :], pdat_im[:, 1, :]
                ma = G[g]["mt"][:, 2 * li:2 * li + 1]
                mb = G[g]["mt"][:, 2 * li + 1:2 * li + 2]
                cr = ch_re[:, 0:wd]
                cm = ch_im[:, 0:wd]
                v.scalar_tensor_tensor(tmp1[g][:, :wd], cm, mb, pev_re,
                                       ALU.mult, ALU.subtract)
                v.scalar_tensor_tensor(tmp2[g][:, :wd], cr, mb, pev_im,
                                       ALU.mult, ALU.add)
                v.scalar_tensor_tensor(pev_re, cr, ma, tmp1[g][:, :wd],
                                       ALU.mult, ALU.subtract)
                v.scalar_tensor_tensor(pev_im, cm, ma, tmp2[g][:, :wd],
                                       ALU.mult, ALU.add)
                v.tensor_copy(pod_re, ch_re[:, 1:1 + wd])
                last = v.tensor_copy(pod_im, ch_im[:, 1:1 + wd])
            t.done(last)
        dep = [t.token]

    tchain = dep[0]
    for g in range(NG):
        out_tasks.append(ctx.dma("qact", clre_d[128 * g:128 * (g + 1), :],
                                 ere[g][:, 0:NCH], deps=[tchain]))
        out_tasks.append(ctx.dma("qact", clim_d[128 * g:128 * (g + 1), :],
                                 eim[g][:, 0:NCH], deps=[tchain]))
        out_tasks.append(ctx.dma("qact", ef_d[128 * g:128 * (g + 1), 0:1],
                                 ere[g][:, NCH:NCH + 1], deps=[tchain],
                                 allow_slow_non_contiguous=True))
        out_tasks.append(ctx.dma("qact", ef_d[128 * g:128 * (g + 1), 1:2],
                                 eim[g][:, NCH:NCH + 1], deps=[tchain],
                                 allow_slow_non_contiguous=True))

    with ctx.btask("sync", out_tasks) as t:
        ctx.wait_all_dmas("sync")
    return nc


def build_pooled(nc, es):
    ctx = Ctx(nc, es)
    x_d = nc.dram_tensor("x", [4096, 1024], F32, kind="ExternalInput").ap()
    p_d = nc.dram_tensor("pooled", [1024, 16], F32, kind="ExternalOutput").ap()
    v = nc.vector
    te = nc.tensor

    xb = [es.enter_context(nc.sbuf_tensor(f"xb{i}", [128, 1024], BF16))
          for i in range(2)]
    xb_b = [Buf(), Buf()]
    ones = es.enter_context(nc.sbuf_tensor("ones", [128, 1], BF16))
    pout = [es.enter_context(nc.sbuf_tensor(f"pout{s}", [128, 16], F32))
            for s in range(8)]
    ps = [es.enter_context(nc.psum_tensor(f"ps{i}", [128, 512], F32))
          for i in range(8)]

    with ctx.task("vector", []) as t:
        t.done(v.memset(ones[:], 1.0 / 256.0))
    t_ones = t.token

    last_mm = None
    for tt in range(4096 // 128):
        c = tt // 2
        half = tt % 2
        bi = tt % 2
        tile, tbuf = xb[bi], xb_b[bi]
        tld = ctx.dma("qpool", tile[:], x_d[128 * tt:128 * (tt + 1), :],
                      deps=tbuf.wdeps())
        tbuf.wrote(tld)
        with ctx.task("tensor", [tld, t_ones]) as t:
            for s in range(8):
                pst = ps[s]
                last_mm = te.matmul(pst[:, c:c + 1],
                                    tile[:, 128 * s:128 * (s + 1)], ones[:],
                                    start=(half == 0), stop=(half == 1),
                                    skip_group_check=True)
            t.done(last_mm)
        tbuf.read(t.token)
    t_mm = t.token

    ev_tasks = []
    for s in range(8):
        pst = ps[s]
        with ctx.task("vector", [t_mm]) as t:
            t.done(v.tensor_copy(pout[s][:], pst[:, 0:16]))
        ev_tasks.append(ctx.dma("qact", p_d[128 * s:128 * (s + 1), :],
                                pout[s][:], deps=[t.token]))

    with ctx.btask("sync", ev_tasks) as t:
        ctx.wait_all_dmas("sync")
    return nc


TT = 512
NT = TOK // TT
NSUB = TT // 128
NG = 8
EPS = 1e-5
NEGBIG = -1e9
SCALE = 0.125
STAGE = 8


def emit_ln_attn(ctx, nc, deps, xin, st, xout, eps_ap, extra_apply_deps=()):
    v = nc.vector
    sc = nc.scalar
    with ctx.task("vector", deps) as t:
        for x, s in zip(xin, st):
            v.bn_stats(s[:, 0:6], x[:, 0:512])
        for x, s in zip(xin, st):
            v.bn_stats(s[:, 6:12], x[:, 512:1024])
        for x, s in zip(xin, st):
            last = v.bn_aggr(s[:, 12:14], s[:, 0:12])
        t.done(last)
    with ctx.task("scalar", [t.token]) as t:
        for s in st:
            sc.activation(s[:, 14:15], s[:, 13:14], AF.Ln, bias=eps_ap)
        for s in st:
            last = sc.activation(s[:, 14:15], s[:, 14:15], AF.Exp, scale=-0.5)
        t.done(last)
    with ctx.task("vector", [t.token]) as t:
        for s in st:
            last = v.scalar_tensor_tensor(s[:, 15:16], s[:, 12:13], -1.0,
                                          s[:, 14:15], ALU.mult, ALU.mult)
        t.done(last)
    with ctx.task("vector", [t.token] + list(extra_apply_deps)) as t:
        for x, s, xo in zip(xin, st, xout):
            last = v.tensor_scalar(xo, x, s[:, 14:15], s[:, 15:16],
                                   ALU.mult, ALU.add)
        t.done(last)
    return t.token


def load_vec_cols_a(ctx, nc, es, name, src1d, n, p=128):
    cols = n // p
    out = es.enter_context(nc.sbuf_tensor(name, [p, cols], F32))
    tasks = []
    for o in range(cols):
        tasks.append(ctx.dma(
            "qsp", out[:, o:o + 1],
            src1d[p * o:p * (o + 1)].rearrange("(q one) -> q one", one=1)))
    return out, tasks


def build_attn(nc, es):
    ctx = Ctx(nc, es)
    def din(name, shape):
        return nc.dram_tensor(name, shape, F32, kind="ExternalInput").ap()
    xh_d = din("xh", [TOK + 128, CH])
    pooled_d = din("pooled", [CH, 64])
    qkvw_d = din("qkv_w", [CH, 3072])
    qkvb_d = din("qkv_b", [3072])
    rqw_d = din("rq_w", [CH, 256])
    rqb_d = din("rq_b", [256])
    rkvw_d = din("rkv_w", [CH, 512])
    rkvb_d = din("rkv_b", [512])
    routw_d = din("rout_w", [256, CH])
    routb_d = din("rout_b", [CH])
    outw_d = din("out_w", [CH, CH])
    outb_d = din("out_b", [CH])
    win_d = din("win", [CH, 64])
    binb_d = din("binb", [64])
    w1emb_d = din("w1emb", [64, CH])
    b1emb_d = din("b1emb", [CH])
    w2emb_d = din("w2emb", [CH, 64])
    b2b_d = din("b2b", [64])
    wout_d = din("wout", [64, CH])
    bout_d = din("bout", [CH])
    gld_d = din("gldiag", [128, 2])
    grd_d = din("grdiag", [128, 2])
    mask_d = din("maskab", [2, 128, 256])
    e2_d = din("e2", [2, 128])
    isf_d = din("isfirst", [128, 1])
    cid0_d = din("cid0", [1, 1])
    y0_d = nc.dram_tensor("y0", [TOK, CH], F32, kind="ExternalOutput").ap()

    v = nc.vector
    sc = nc.scalar
    te = nc.tensor
    gp = nc.gpsimd

    PS0 = es.enter_context(nc.psum_tensor("PS0", [128, 1024], F32))
    PS1 = es.enter_context(nc.psum_tensor("PS1", [128, 1024], F32))
    PSq = [Buf() for _ in range(4)]

    def psq(i):
        return (PS0 if i < 2 else PS1), 512 * (i % 2)

    def pscol(c):
        return (PS0 if c < 1024 else PS1), (c % 1024)
    PV = es.enter_context(nc.psum_tensor("PV", [128, 1024], F32))
    PVh = [Buf(), Buf()]
    PB = es.enter_context(nc.psum_tensor("PB", [128, 1024], BF16))
    PBb = Buf()
    PT = es.enter_context(nc.psum_tensor("PT", [128, 512], F32))
    PTb = Buf()

    ident = es.enter_context(nc.sbuf_tensor("ident", [128, 128], F32))
    identb = es.enter_context(nc.sbuf_tensor("identb", [128, 128], BF16))
    ii = es.enter_context(nc.sbuf_tensor("ii", [128, 128], I32))
    ci = es.enter_context(nc.sbuf_tensor("ci", [128, 128], I32))
    iota256i = es.enter_context(nc.sbuf_tensor("iota256i", [1, 256], I32))
    iota256 = es.enter_context(nc.sbuf_tensor("iota256", [1, 256], F32))
    with ctx.task("gpsimd", []) as t:
        gp.iota(ci[:], pattern=[[1, 128]], base=0, channel_multiplier=0)
        gp.iota(ii[:], pattern=[[0, 128]], base=0, channel_multiplier=1)
        t.done(gp.iota(iota256i[:], pattern=[[0, 4], [1, 64]], base=0,
                       channel_multiplier=0))
    with ctx.task("vector", [t.token]) as t:
        v.tensor_tensor(ii[:], ci[:], ii[:], ALU.is_equal)
        v.tensor_copy(iota256[:], iota256i[:])
        v.tensor_copy(ident[:], ii[:])
        t.done(v.tensor_copy(identb[:], ii[:]))
    t_ident = t.token
    epst = es.enter_context(nc.sbuf_tensor("epst", [128, 1], F32))
    onesp = es.enter_context(nc.sbuf_tensor("onesp", [65, 128], BF16))
    ones1 = onesp[0:1, :]
    ones32 = onesp[32:33, :]
    ones64 = onesp[64:65, :]
    with ctx.task("vector", []) as t:
        v.memset(epst[:], EPS)
        v.memset(onesp[0:1, :], 1.0)
        v.memset(onesp[32:33, :], 1.0)
        t.done(v.memset(onesp[64:65, :], 1.0))
    t_eps = t.token

    wld = []
    def wload(name, shape, src_ap):
        tns = es.enter_context(nc.sbuf_tensor(name, shape, BF16))
        wld.append(ctx.dma("qpool", tns[:], src_ap))
        return tns
    qkvw = [wload(f"qkvw{g}", [128, 3072], qkvw_d[128 * g:128 * (g + 1), :])
            for g in range(NG)]
    rqw = [wload(f"rqw{g}", [128, 256], rqw_d[128 * g:128 * (g + 1), :])
           for g in range(NG)]
    rkvw = [wload(f"rkvw{g}", [128, 512], rkvw_d[128 * g:128 * (g + 1), :])
            for g in range(NG)]
    routw = [wload(f"routw{c}", [128, CH], routw_d[128 * c:128 * (c + 1), :])
             for c in range(2)]
    outw = [wload(f"outw{g}", [128, CH], outw_d[128 * g:128 * (g + 1), :])
            for g in range(NG)]
    winw = [wload(f"winw{g}", [128, 64], win_d[128 * g:128 * (g + 1), :])
            for g in range(NG)]
    w1emb = wload("w1embw", [64, CH], w1emb_d[:])
    w2emb = [wload(f"w2embw{g}", [128, 64], w2emb_d[128 * g:128 * (g + 1), :])
             for g in range(NG)]
    woutw = wload("woutw", [64, CH], wout_d[:])
    gld = wload("gldw", [128, 2], gld_d[:])
    grd = wload("grdw", [128, 2], grd_d[:])
    e2b = wload("e2b", [2, 128], e2_d[:])
    brows = es.enter_context(nc.sbuf_tensor("brows", [65, CH], BF16))
    wld.append(ctx.dma("qpool", brows[0:1, :],
                       qkvb_d[2048:3072].rearrange("(one f) -> one f", one=1)))
    wld.append(ctx.dma("qpool", brows[32:33, :],
                       outb_d[:].rearrange("(one f) -> one f", one=1)))
    wld.append(ctx.dma("qpool", brows[64:65, :],
                       bout_d[:].rearrange("(one f) -> one f", one=1)))
    vbrow = brows[0:1, :]
    outbrow = brows[32:33, :]
    boutrow = brows[64:65, :]
    t_wld = ctx.barrier_task("sync", wld)

    if STAGE == -1:
        stg = es.enter_context(nc.sbuf_tensor("stg", [128, 64], F32))
        with ctx.task("vector", [t_wld, t_eps, t_ident]) as t:
            t.done(v.tensor_copy(stg[:], qkvw[0][:, 0:64]))
        tst = ctx.dma("qact", y0_d[0:128, 0:64], stg[:], deps=[t.token])
        with ctx.btask("sync", [tst]) as t:
            ctx.wait_all_dmas("sync")
        return nc

    qkb_col, tb1 = load_vec_cols_a(ctx, nc, es, "qkbcol", qkvb_d[0:2048], 2048)
    rqb_col, tb2 = load_vec_cols_a(ctx, nc, es, "rqbcol", rqb_d[:], 256)
    rkvb_col, tb3 = load_vec_cols_a(ctx, nc, es, "rkvbcol", rkvb_d[:], 512)
    routb_col, tb4 = load_vec_cols_a(ctx, nc, es, "routbcol", routb_d[:], CH)
    b1e_col, tb6 = load_vec_cols_a(ctx, nc, es, "b1ecol", b1emb_d[:], CH)
    binb_col, tb8 = load_vec_cols_a(ctx, nc, es, "binbcol", binb_d[:], 64, p=64)
    b2b_col, tb9 = load_vec_cols_a(ctx, nc, es, "b2bcol", b2b_d[:], 64, p=64)
    t_bias = ctx.barrier_task(
        "sync", tb1 + tb2 + tb3 + tb4 + tb6 + tb8 + tb9)

    maskn = es.enter_context(nc.sbuf_tensor("maskn", [128, 256], BF16))
    maskq0 = es.enter_context(nc.sbuf_tensor("maskq0", [128, 256], BF16))
    mstage = es.enter_context(nc.sbuf_tensor("mstage", [128, 512], F32))
    isf = es.enter_context(nc.sbuf_tensor("isfs", [128, 1], F32))
    cid0 = es.enter_context(nc.sbuf_tensor("cid0s", [1, 1], F32))
    tmm = [ctx.dma("qsp", mstage[:, 0:256], mask_d[0]),
           ctx.dma("qsp", mstage[:, 256:512], mask_d[1]),
           ctx.dma("qsp", isf[:], isf_d[:]),
           ctx.dma("qsp", cid0[:], cid0_d[:])]
    with ctx.task("vector", tmm) as t:
        v.tensor_copy(maskn[:], mstage[:, 0:256])
        v.tensor_sub(mstage[:, 256:512], mstage[:, 256:512], mstage[:, 0:256])
        v.scalar_tensor_tensor(mstage[:, 256:512], mstage[:, 256:512], isf[:],
                               mstage[:, 0:256], ALU.mult, ALU.add)
        t.done(v.tensor_copy(maskq0[:], mstage[:, 256:512]))
    t_mask = t.token

    pooledb = [es.enter_context(nc.sbuf_tensor(f"pooledb{g}", [128, 64], BF16))
               for g in range(NG)]
    pld = [ctx.dma("qpool", pooledb[g][:], pooled_d[128 * g:128 * (g + 1), :])
           for g in range(NG)]
    t_pool = ctx.barrier_task("sync", pld)

    if STAGE == -2:
        with ctx.task("vector", [t_wld, t_bias, t_mask, t_pool, t_eps,
                                 t_ident]) as t:
            t.done(v.tensor_copy(mstage[:, 0:16], qkb_col[:, 0:16]))
        tst = ctx.dma("qact", y0_d[0:128, 0:16], mstage[:, 0:16],
                      deps=[t.token])
        with ctx.btask("sync", [tst]) as t:
            ctx.wait_all_dmas("sync")
        return nc

    rkvT = [es.enter_context(nc.sbuf_tensor(f"rkvT{j}", [128, 64], BF16))
            for j in range(4)]
    rvtm = es.enter_context(nc.sbuf_tensor("rvtm", [64, 256], BF16))
    rkv_tasks = []
    for j in range(4):
        qb = PSq[j % 2]
        with ctx.task("tensor", [t_wld, t_pool] + qb.wdeps()) as t:
            pst, pc = psq(j % 2)
            for g in range(NG):
                last = te.matmul(pst[:, pc:pc + 64],
                                 rkvw[g][:, 128 * j:128 * (j + 1)],
                                 pooledb[g][:], start=(g == 0), stop=(g == 7))
            t.done(last)
        qb.wrote(t.token)
        with ctx.task("vector", [t.token, t_bias]) as t2:
            t2.done(v.tensor_scalar_add(rkvT[j][:], pst[:, pc:pc + 64],
                                        rkvb_col[:, j:j + 1]))
        qb.read(t2.token)
        rkv_tasks.append(t2.token)
    if STAGE == 0:
        with ctx.task("vector", rkv_tasks + [t_mask, t_eps]) as t:
            t.done(v.tensor_copy(mstage[:, 0:64], rkvT[0][:]))
        tst = ctx.dma("qact", y0_d[0:128, 0:64], mstage[:, 0:64],
                      deps=[t.token])
        with ctx.btask("sync", [tst]) as t:
            ctx.wait_all_dmas("sync")
        return nc

    with ctx.task("tensor", rkv_tasks[2:] + [t_ident] + PBb.wdeps()) as t:
        for c in range(2):
            last = te.transpose(PB[0:64, 128 * c:128 * (c + 1)],
                                rkvT[2 + c][:], identb[:])
        t.done(last)
    PBb.wrote(t.token)
    with ctx.task("vector", [t.token]) as t2:
        t2.done(v.tensor_copy(rvtm[:], PB[0:64, 0:256]))
    PBb.read(t2.token)
    t_rv = t2.token

    x_tm = [es.enter_context(nc.sbuf_tensor(f"xtm{q}", [128, CH], F32))
            for q in range(NSUB)]
    x_tm_b = [Buf() for _ in range(NSUB)]
    xb16 = [es.enter_context(nc.sbuf_tensor(f"xb16_{i}", [128, CH], BF16))
            for i in range(1)]
    xb16_b = [Buf()]
    xfm = es.enter_context(nc.sbuf_tensor("xfm", [128, NG * TT], BF16))
    xfm_b = Buf()
    xfm_g = [xfm[:, TT * g:TT * (g + 1)] for g in range(NG)]
    xhfm = es.enter_context(nc.sbuf_tensor("xhfm", [128, NG * 128], BF16))
    xhfm_b = Buf()
    qfm = es.enter_context(nc.sbuf_tensor("qfm", [128, NG * TT], BF16))
    qfm_b = Buf()
    kfm = es.enter_context(nc.sbuf_tensor("kfm", [128, NG * 640], BF16))
    kfm_b = Buf()
    vring = [es.enter_context(nc.sbuf_tensor(f"vring{r}", [128, CH], BF16))
             for r in range(5)]
    vring_b = [Buf() for _ in range(5)]
    rqfm = es.enter_context(nc.sbuf_tensor("rqfm", [128, 2 * TT], BF16))
    rqfm_b = Buf()
    lattn = es.enter_context(nc.sbuf_tensor("lattn", [128, 256], BF16))
    lattn_b = Buf()
    lattnT = es.enter_context(nc.sbuf_tensor("lattnT", [64, 512], BF16))
    lattnT_b = Buf()
    latfm = [es.enter_context(nc.sbuf_tensor(f"latfm{c}", [128, 128], BF16))
             for c in range(2)]
    latfm_b = [Buf(), Buf()]
    mrow = es.enter_context(nc.sbuf_tensor("mrow", [1, 256], F32))
    mrowb = es.enter_context(nc.sbuf_tensor("mrowb", [1, 256], BF16))
    mrow_b = Buf()
    lsums = es.enter_context(nc.sbuf_tensor("lsums", [128, 24], F32))
    remote = es.enter_context(nc.sbuf_tensor("remote", [128, 1024], BF16))
    remote_b = Buf()
    attn = es.enter_context(nc.sbuf_tensor("attn", [128, 4096], BF16))
    attn_b = [Buf(), Buf()]
    attnT = es.enter_context(nc.sbuf_tensor("attnT", [128, 4096], BF16))
    attnT_b = Buf()
    sums = es.enter_context(nc.sbuf_tensor("sums", [128, 48], F32))
    localsb = es.enter_context(nc.sbuf_tensor("localsb", [128, 1024], BF16))
    localsb_b = Buf()
    gates = es.enter_context(nc.sbuf_tensor("gates", [2, 1024], BF16))
    gates_b = Buf()
    mixed = es.enter_context(nc.sbuf_tensor("mixed", [128, NG * TT], BF16))
    mixed_b = Buf()
    x1 = [es.enter_context(nc.sbuf_tensor(f"x1_{q}", [128, CH], F32))
          for q in range(NSUB)]
    x1_b = [Buf() for _ in range(NSUB)]
    x1fm = xfm
    x1fm_b = xfm_b
    s_sb = es.enter_context(nc.sbuf_tensor("s_sb", [64, TT], BF16))
    s_sb_b = Buf()
    h_sb = attnT
    h_sb_b = attnT_b
    g_sb = es.enter_context(nc.sbuf_tensor("g_sb", [64, TT], BF16))
    g_sb_b = Buf()
    st1 = [es.enter_context(nc.sbuf_tensor(f"st1{q}", [128, 16], F32))
           for q in range(NSUB)]
    st2 = st1

    out_tasks = []

    def tbatch(srcs, dst_view, deps, bf16=True):
        pp, ppb, idt = (PB, PBb, identb) if bf16 else (PT, PTb, ident)
        with ctx.task("tensor", deps + [t_ident] + ppb.wdeps()) as t:
            for i, s in enumerate(srcs):
                last = te.transpose(pp[:, 128 * i:128 * (i + 1)], s, idt[:])
            t.done(last)
        ppb.wrote(t.token)
        with ctx.task("vector", [t.token]) as t2:
            t2.done(v.tensor_copy(dst_view, pp[:, 0:128 * len(srcs)]))
        ppb.read(t2.token)
        return t2.token

    for ti in range(NT):
        tok0 = ti * TT

        xsub = []
        for q in range(NSUB):
            tld = ctx.dma("qsp", x_tm[q][:],
                          xh_d[128 + tok0 + 128 * q:256 + tok0 + 128 * q, :],
                          deps=x_tm_b[q].wdeps())
            x_tm_b[q].wrote(tld)
            bi = 0
            tldb = ctx.dma("qpool", xb16[bi][:],
                           xh_d[128 + tok0 + 128 * q:256 + tok0 + 128 * q, :],
                           deps=xb16_b[bi].wdeps())
            xb16_b[bi].wrote(tldb)
            dst = xfm[:].rearrange("p (g c) -> p g c", c=TT)[:, :,
                                                            128 * q:128 * (q + 1)]
            tev = tbatch([xb16[bi][:, 128 * g:128 * (g + 1)] for g in range(NG)],
                         dst, [tldb] + (xfm_b.wdeps() if q == 0 else []))
            xb16_b[bi].read(tev)
            xsub.append(tev)
        xfm_b.wrote(ctx.barrier_task("vector", xsub))
        if ti == 0:
            bi = 0
            tldb = ctx.dma("qpool", xb16[bi][:], xh_d[0:128, :],
                           deps=xb16_b[bi].wdeps())
            xb16_b[bi].wrote(tldb)
            tev = tbatch([xb16[bi][:, 128 * g:128 * (g + 1)] for g in range(NG)],
                         xhfm[:], [tldb] + xhfm_b.wdeps())
            xb16_b[bi].read(tev)
            xhfm_b.wrote(tev)

        if ti == 0:
            for o in range(NG):
                qb = PSq[o % 2]
                pst, pc = psq(o % 2)
                with ctx.task("tensor", [t_wld] + qb.wdeps() +
                              xhfm_b.rdeps()) as t:
                    for g in range(NG):
                        last = te.matmul(
                            pst[:, pc:pc + 128],
                            qkvw[g][:, 1024 + 128 * o:1152 + 128 * o],
                            xhfm[:, 128 * g:128 * (g + 1)],
                            start=(g == 0), stop=(g == 7))
                    t.done(last)
                xhfm_b.read(t.token)
                qb.wrote(t.token)
                with ctx.task("vector", [t.token, t_bias] +
                              (kfm_b.wdeps() if o == 0 else [])) as t2:
                    t2.done(v.tensor_scalar_add(
                        kfm[:, 640 * o:640 * o + 128],
                        pst[:, pc:pc + 128],
                        qkb_col[:, 8 + o:9 + o]))
                qb.read(t2.token)
                kfm_b.wrote(t2.token)
        else:
            with ctx.task("vector", kfm_b.wdeps()) as t2:
                for o in range(NG):
                    last = v.tensor_copy(kfm[:, 640 * o:640 * o + 128],
                                         kfm[:, 640 * o + 512:640 * o + 640])
                t2.done(last)
            kfm_b.wrote(t2.token)
        proj = [("q", 0, NG, 0, qfm, qfm_b, qkb_col, 0, qkvw, 512, 0),
                ("k", 0, NG, 1024, kfm, kfm_b, qkb_col, 8, qkvw, 640, 128),
                ("rq", 0, 2, 0, rqfm, rqfm_b, rqb_col, 0, rqw, 512, 0)]
        for name, o0, no, wcol0, dstt, dstb, bcol, b0, wt, stride, coff in proj:
            for o in range(no):
                qb = PSq[2 + (o % 2)]
                pst, pc = psq(2 + (o % 2))
                with ctx.task("tensor", [t_wld] + qb.wdeps() +
                              xfm_b.rdeps()) as t:
                    for g in range(NG):
                        last = te.matmul(
                            pst[:, pc:pc + 512],
                            wt[g][:, wcol0 + 128 * o:wcol0 + 128 * o + 128],
                            xfm_g[g], start=(g == 0), stop=(g == 7))
                    t.done(last)
                xfm_b.read(t.token)
                qb.wrote(t.token)
                with ctx.task("vector", [t.token, t_bias] + dstb.wdeps()) as t2:
                    t2.done(v.tensor_scalar_add(
                        dstt[:, stride * o + coff:stride * o + coff + 512],
                        pst[:, pc:pc + 512],
                        bcol[:, b0 + o:b0 + o + 1]))
                qb.read(t2.token)
                dstb.wrote(t2.token)
        vsubs = ([-1] if ti == 0 else []) + list(range(NSUB))
        for j in vsubs:
            m = ti * NSUB + j
            r = (m + 1) % 5
            for half in range(2):
                hb = PVh[half]
                with ctx.task("tensor", [t_wld] + hb.wdeps() +
                              (xfm_b.rdeps() if j >= 0 else
                               xhfm_b.rdeps())) as t:
                    for g in range(NG):
                        rr = (xfm_g[g][:, 128 * j:128 * (j + 1)] if j >= 0
                              else xhfm[:, 128 * g:128 * (g + 1)])
                        te.matmul(PV[:, 512 * half:512 * half + 512], rr,
                                  qkvw[g][:, 2048 + 512 * half:2560 + 512 * half],
                                  start=(g == 0), stop=False)
                    last = te.matmul(PV[:, 512 * half:512 * half + 512],
                                     ones1,
                                     vbrow[:, 512 * half:512 * half + 512],
                                     start=False, stop=True)
                    t.done(last)
                (xfm_b if j >= 0 else xhfm_b).read(t.token)
                hb.wrote(t.token)
                with ctx.task("vector", [t.token] +
                              (vring_b[r].wdeps() if half == 0 else [])) as t2:
                    t2.done(v.tensor_copy(
                        vring[r][:, 512 * half:512 * half + 512],
                        PV[:, 512 * half:512 * half + 512]))
                hb.read(t2.token)
                vring_b[r].wrote(t2.token)

        for q in range(NSUB if STAGE >= 2 else 0):
            chunkidx = (ti * NSUB + q) // 2
            m = ti * NSUB + q
            with ctx.task("vector", [t_ident] + mrow_b.wdeps() + [t_mask]) as t:
                v.tensor_single_scalar(mrow[:], iota256[:], -float(chunkidx),
                                       ALU.add)
                v.tensor_single_scalar(mrow[:], mrow[:], cid0[:, 0:1],
                                       ALU.is_ge)
                v.tensor_single_scalar(mrow[:], mrow[:], NEGBIG, ALU.mult)
                t.done(v.tensor_copy(mrowb[:], mrow[:]))
            mrow_b.wrote(t.token)
            hb = PVh[0]
            with ctx.task("tensor", hb.wdeps() + mrow_b.rdeps() +
                          rqfm_b.rdeps() + rkv_tasks) as t:
                for h in range(4):
                    rqh = rqfm[64 * (h % 2):64 * (h % 2) + 64,
                               512 * (h // 2) + 128 * q:
                               512 * (h // 2) + 128 * (q + 1)]
                    rkh = rkvT[h // 2][64 * (h % 2):64 * (h % 2) + 64, :]
                    te.matmul(PV[:, 64 * h:64 * (h + 1)], rqh, rkh,
                              start=True, stop=False, skip_group_check=True)
                    last = te.matmul(PV[:, 64 * h:64 * (h + 1)], ones1,
                                     mrowb[:, 64 * h:64 * (h + 1)],
                                     start=False, stop=True,
                                     skip_group_check=True)
                t.done(last)
            rqfm_b.read(t.token)
            mrow_b.read(t.token)
            hb.wrote(t.token)
            with ctx.task("scalar", [t.token] + lattn_b.wdeps()) as t2:
                for h in range(4):
                    last = sc.activation(lattn[:, 64 * h:64 * (h + 1)],
                                         PV[:, 64 * h:64 * (h + 1)], AF.Exp,
                                         scale=SCALE,
                                         accum_out=lsums[:, h:h + 1])
                t2.done(last)
            hb.read(t2.token)
            lattn_b.wrote(t2.token)
            with ctx.task("vector", lattn_b.rdeps()) as t4:
                v.tensor_single_scalar(lsums[:, 8:12], lsums[:, 0:4], 1e-30,
                                       ALU.max)
                v.memset(lsums[:, 16:20], 0.0)
                v.reciprocal(lsums[:, 4:8], lsums[:, 8:12])
                v.memset(lsums[:, 16:20], 0.0)
                t4.done(v.memset(lsums[:, 20:24], 0.0))
            with ctx.task("vector", [t4.token] + lattn_b.wdeps()) as t5:
                for h in range(4):
                    last = v.tensor_scalar_mul(lattn[:, 64 * h:64 * (h + 1)],
                                               lattn[:, 64 * h:64 * (h + 1)],
                                               lsums[:, 4 + h:5 + h])
                t5.done(last)
            lattn_b.wrote(t5.token)
            with ctx.task("tensor", lattn_b.rdeps() + [t_ident] +
                          PBb.wdeps()) as t:
                for h in range(4):
                    last = te.transpose(PB[0:64, 128 * h:128 * (h + 1)],
                                        lattn[:, 64 * h:64 * (h + 1)],
                                        identb[:])
                t.done(last)
            lattn_b.read(t.token)
            PBb.wrote(t.token)
            with ctx.task("vector", [t.token] + lattnT_b.wdeps()) as t2:
                t2.done(v.tensor_copy(lattnT[:], PB[0:64, 0:512]))
            PBb.read(t2.token)
            lattnT_b.wrote(t2.token)
            hb2 = PVh[1]
            with ctx.task("tensor", hb2.wdeps() + lattnT_b.rdeps() +
                          [t_rv]) as t:
                for h in range(4):
                    last = te.matmul(PV[0:64, 512 + 128 * h:640 + 128 * h],
                                     rvtm[:, 64 * h:64 * (h + 1)],
                                     lattnT[:, 128 * h:128 * (h + 1)],
                                     start=True, stop=True,
                                     skip_group_check=True)
                t.done(last)
            lattnT_b.read(t.token)
            hb2.wrote(t.token)
            lf = []
            for h in range(4):
                ci2 = h // 2
                with ctx.task("vector", [t.token] +
                              (latfm_b[ci2].wdeps() if h % 2 == 0 else [])) as t2:
                    t2.done(v.tensor_copy(
                        latfm[ci2][64 * (h % 2):64 * (h % 2) + 64, :],
                        PV[0:64, 512 + 128 * h:640 + 128 * h]))
                hb2.read(t2.token)
                latfm_b[ci2].wrote(t2.token)
            for sl in range(NG):
                hb3 = PVh[sl // 4]
                pvap = PV[:, 128 * sl:128 * (sl + 1)]
                with ctx.task("tensor", hb3.wdeps() + [t_wld] +
                              latfm_b[0].rdeps() + latfm_b[1].rdeps()) as t:
                    for c in range(2):
                        last = te.matmul(pvap,
                                         routw[c][:, 128 * sl:128 * (sl + 1)],
                                         latfm[c][:], start=(c == 0),
                                         stop=(c == 1))
                    t.done(last)
                latfm_b[0].read(t.token)
                latfm_b[1].read(t.token)
                hb3.wrote(t.token)
                with ctx.task("vector", [t.token, t_bias] +
                              (remote_b.wdeps() if sl == 0 else [])) as t2:
                    t2.done(v.tensor_scalar_add(
                        remote[:, 128 * sl:128 * (sl + 1)], pvap,
                        routb_col[:, sl:sl + 1]))
                hb3.read(t2.token)
                remote_b.wrote(t2.token)

            if STAGE < 3:
                continue
            mask_tile = maskq0 if (ti == 0 and q == 0) else maskn
            for half in range(2):
                with ctx.task("tensor",
                              sum((PSq[i].wdeps() for i in range(4)), []) +
                              [t_wld, t_mask] + qfm_b.rdeps() +
                              kfm_b.rdeps()) as t:
                    for hh in range(8):
                        h = half * 8 + hh
                        qh = qfm[64 * (h % 2):64 * (h % 2) + 64,
                                 512 * (h // 2) + 128 * q:
                                 512 * (h // 2) + 128 * (q + 1)]
                        kh = kfm[64 * (h % 2):64 * (h % 2) + 64,
                                 640 * (h // 2) + 128 * q:
                                 640 * (h // 2) + 128 * q + 256]
                        spt, spc = pscol(256 * hh)
                        te.matmul(spt[:, spc:spc + 256], qh, kh,
                                  start=True, stop=False, skip_group_check=True)
                        last = te.matmul(spt[:, spc:spc + 256],
                                         identb[:], mask_tile[:], start=False,
                                         stop=True, skip_group_check=True)
                    t.done(last)
                qfm_b.read(t.token)
                kfm_b.read(t.token)
                for i in range(4):
                    PSq[i].wrote(t.token)
                if STAGE == 9 and ti == 0 and q == 3 and half == 1:
                    with ctx.task("vector", [t.token]) as tdg:
                        tdg.done(v.tensor_copy(mstage[:, 0:512],
                                               PS1[:, 0:512]))
                    for i in range(4):
                        PSq[i].read(tdg.token)
                    out_tasks.append(ctx.dma("qact", y0_d[1408:1536, 0:512],
                                             mstage[:, 0:512],
                                             deps=[tdg.token]))
                ab = attn_b[half]
                with ctx.task("scalar", [t.token] + ab.wdeps()) as t2:
                    for hh in range(8):
                        spt = PS0 if hh < 4 else PS1
                        spc = 256 * (hh % 4)
                        h = half * 8 + hh
                        last = sc.activation(attn[:, 256 * h:256 * (h + 1)],
                                             spt[:, spc:spc + 256], AF.Exp,
                                             scale=SCALE,
                                             accum_out=sums[:, h:h + 1])
                    t2.done(last)
                for i in range(4):
                    PSq[i].read(t2.token)
                ab.wrote(t2.token)
                if STAGE == 9 and ti == 0 and q == 3 and half == 1:
                    out_tasks.append(ctx.dma("qpool", y0_d[1536:1664, :],
                                             attn[:, 3072:4096],
                                             deps=[t2.token]))
                    ab.read(out_tasks[-1])
            with ctx.task("vector", attn_b[0].rdeps() + attn_b[1].rdeps()) as t2:
                v.tensor_single_scalar(sums[:, 16:32], sums[:, 0:16], 1e-30,
                                       ALU.max)
                v.memset(sums[:, 32:48], 0.0)
                v.reciprocal(sums[:, 0:16], sums[:, 16:32])
                v.memset(sums[:, 32:48], 0.0)
                t2.done(v.memset(sums[:, 32:48], 0.0))
            with ctx.task("vector", [t2.token] + attn_b[0].wdeps() +
                          attn_b[1].wdeps()) as t3:
                for h in range(16):
                    last = v.tensor_scalar_mul(attn[:, 256 * h:256 * (h + 1)],
                                               attn[:, 256 * h:256 * (h + 1)],
                                               sums[:, h:h + 1])
                t3.done(last)
            attn_b[0].wrote(t3.token)
            attn_b[1].wrote(t3.token)
            atts = []
            for blk in range(4):
                srcs = [attn[:, 128 * (8 * blk + i):128 * (8 * blk + i + 1)]
                        for i in range(8)]
                tev = tbatch(srcs, attnT[:, 1024 * blk:1024 * (blk + 1)],
                             [t3.token] + (attnT_b.wdeps() if blk == 0 else []))
                atts.append(tev)
            attnT_b.wrote(ctx.barrier_task("vector", atts))
            for sl in range(NG):
                hb4 = PVh[sl // 4]
                with ctx.task("tensor", hb4.wdeps() + remote_b.rdeps() +
                              attnT_b.rdeps() +
                              [vring_b[m % 5].writer,
                               vring_b[(m + 1) % 5].writer]) as t:
                    for h2 in range(2):
                        h = 2 * sl + h2
                        for ic, r in enumerate(((m) % 5, (m + 1) % 5)):
                            last = te.matmul(
                                PV[64 * h2:64 * h2 + 64,
                                   128 * sl:128 * (sl + 1)],
                                vring[r][:, 64 * h:64 * (h + 1)],
                                attnT[:, 128 * (2 * h + ic):
                                      128 * (2 * h + ic) + 128],
                                start=(ic == 0), stop=(ic == 1),
                                skip_group_check=True,
                                tile_position=(0, 64 * h2))
                    t.done(last)
                vring_b[m % 5].read(t.token)
                vring_b[(m + 1) % 5].read(t.token)
                attnT_b.read(t.token)
                hb4.wrote(t.token)
            with ctx.task("vector", [PVh[0].writer, PVh[1].writer] +
                          localsb_b.wdeps()) as t:
                v.tensor_copy(localsb[:, 0:512], PV[:, 0:512])
                t.done(v.tensor_copy(localsb[:, 512:1024], PV[:, 512:1024]))
            PVh[0].read(t.token)
            PVh[1].read(t.token)
            localsb_b.wrote(t.token)
            with ctx.task("tensor",
                          sum((PSq[i].wdeps() for i in range(2)), []) +
                          localsb_b.rdeps() + remote_b.rdeps() + [t_wld]) as t:
                for sl in range(NG):
                    te.matmul(PS0[0:2, 128 * sl:128 * (sl + 1)], gld[:],
                              localsb[:, 128 * sl:128 * (sl + 1)],
                              start=True, stop=False, skip_group_check=True)
                    last = te.matmul(PS0[0:2, 128 * sl:128 * (sl + 1)], grd[:],
                                     remote[:, 128 * sl:128 * (sl + 1)],
                                     start=False, stop=True,
                                     skip_group_check=True)
                t.done(last)
            localsb_b.read(t.token)
            remote_b.read(t.token)
            for i in range(2):
                PSq[i].wrote(t.token)
            with ctx.task("scalar", [t.token] + gates_b.wdeps()) as t2:
                sc.activation(gates[:, 0:512], PS0[0:2, 0:512], AF.Sigmoid)
                t2.done(sc.activation(gates[:, 512:1024], PS0[0:2, 512:1024],
                                      AF.Sigmoid))
            for i in range(2):
                PSq[i].read(t2.token)
            gates_b.wrote(t2.token)
            with ctx.task("tensor",
                          sum((PSq[i].wdeps() for i in range(2)), []) +
                          gates_b.rdeps() + [t_wld]) as t:
                for sl in range(NG):
                    last = te.matmul(PS0[:, 128 * sl:128 * (sl + 1)], e2b[:],
                                     gates[:, 128 * sl:128 * (sl + 1)],
                                     start=True, stop=True,
                                     skip_group_check=True)
                t.done(last)
            gates_b.read(t.token)
            for i in range(2):
                PSq[i].wrote(t.token)
            t_gb = t.token
            with ctx.task("gpsimd", localsb_b.rdeps() + remote_b.rdeps() +
                          localsb_b.wdeps()) as t:
                t.done(gp.tensor_sub(localsb[:], localsb[:], remote[:]))
            remote_b.read(t.token)
            localsb_b.wrote(t.token)
            with ctx.task("vector", [t.token, t_gb] + localsb_b.wdeps()) as t2:
                t2.done(v.scalar_tensor_tensor(localsb[:], PS0[:, 0:1024], 1.0,
                                               localsb[:], ALU.mult, ALU.mult))
            for i in range(2):
                PSq[i].read(t2.token)
            localsb_b.wrote(t2.token)
            mx = mixed[:].rearrange("p (s c) -> p s c", c=TT)[:, :,
                                                             128 * q:128 * (q + 1)]
            mxsrc = localsb[:].rearrange("p (s c) -> p s c", c=128)
            rsrc = remote[:].rearrange("p (s c) -> p s c", c=128)
            with ctx.task("gpsimd", localsb_b.rdeps() + remote_b.rdeps() +
                          (mixed_b.wdeps() if q == 0 else [])) as t3:
                t3.done(gp.tensor_add(mx, mxsrc, rsrc))
            localsb_b.read(t3.token)
            remote_b.read(t3.token)
            mixed_b.wrote(t3.token)

        if STAGE == 9 and ti == 0:
            tapdeps = dict(localsb=(localsb, localsb_b),
                           remote=(remote, remote_b),
                           mixedsl=(None, mixed_b))
            tt1 = ctx.dma("qpool", y0_d[512:640, :], localsb[:],
                          deps=localsb_b.rdeps())
            localsb_b.read(tt1)
            tt2 = ctx.dma("qpool", y0_d[640:768, :], remote[:],
                          deps=remote_b.rdeps())
            remote_b.read(tt2)
            tt3 = ctx.dma("qpool", y0_d[768:896, :],
                          mixed[:, 0:1024], deps=mixed_b.rdeps())
            mixed_b.read(tt3)
            tt4 = ctx.dma("qact", y0_d[896:1024, 0:48], sums[:],
                          deps=attn_b[1].wdeps())
            tt5 = ctx.dma("qpool", y0_d[1024:1152, :],
                          attn[:, 3072:4096], deps=attn_b[1].rdeps())
            attn_b[1].read(tt5)
            tt6 = ctx.dma("qpool", y0_d[1152:1280, :],
                          qfm[:, 3072:4096], deps=qfm_b.rdeps())
            qfm_b.read(tt6)
            tt7 = ctx.dma("qpool", y0_d[1280:1408, 0:640],
                          kfm[:, 3840:4480], deps=kfm_b.rdeps())
            kfm_b.read(tt7)
            out_tasks += [tt1, tt2, tt3, tt4, tt5, tt6, tt7]
        if STAGE < 4:
            for q in range(NSUB):
                tst = ctx.dma("qact",
                              y0_d[tok0 + 128 * q:tok0 + 128 * (q + 1), :],
                              x_tm[q][:], deps=x_tm_b[q].rdeps())
                x_tm_b[q].read(tst)
                out_tasks.append(tst)
            continue
        for q in range(NSUB):
            for half in range(2):
                hb = PVh[half]
                with ctx.task("tensor", hb.wdeps() + [t_wld] +
                              mixed_b.rdeps()) as t:
                    for sl in range(NG):
                        te.matmul(PV[:, 512 * half:512 * half + 512],
                                  mixed[:, TT * sl + 128 * q:
                                        TT * sl + 128 * (q + 1)],
                                  outw[sl][:, 512 * half:512 * half + 512],
                                  start=(sl == 0), stop=False)
                    last = te.matmul(PV[:, 512 * half:512 * half + 512],
                                     ones32,
                                     outbrow[:, 512 * half:512 * half + 512],
                                     start=False, stop=True)
                    t.done(last)
                mixed_b.read(t.token)
                hb.wrote(t.token)
                with ctx.task("vector", [t.token] + x_tm_b[q].rdeps() +
                              x_tm_b[q].wdeps()) as t2:
                    t2.done(v.tensor_add(
                        x_tm[q][:, 512 * half:512 * (half + 1)],
                        x_tm[q][:, 512 * half:512 * (half + 1)],
                        PV[:, 512 * half:512 * half + 512]))
                hb.read(t2.token)
                x_tm_b[q].wrote(t2.token)
        t_ln1 = emit_ln_attn(ctx, nc, [x_tm_b[q].writer for q in range(NSUB)],
                        [x_tm[q][:] for q in range(NSUB)], st1,
                        [x1[q][:] for q in range(NSUB)], epst[:, 0:1],
                        extra_apply_deps=[t_eps] +
                        [tk for q in range(NSUB) for tk in x1_b[q].wdeps()])
        for q in range(NSUB):
            x_tm_b[q].read(t_ln1)
            x1_b[q].wrote(t_ln1)

        x1subs = []
        for q in range(NSUB):
            for half in range(2):
                with ctx.task("tensor", [t_ln1, t_ident] + PTb.wdeps() +
                              (x1fm_b.wdeps() if q == 0 and half == 0
                               else [])) as t:
                    for i in range(4):
                        g = half * 4 + i
                        last = te.transpose(PT[:, 128 * i:128 * (i + 1)],
                                            x1[q][:, 128 * g:128 * (g + 1)],
                                            ident[:])
                    t.done(last)
                x1_b[q].read(t.token)
                PTb.wrote(t.token)
                dst = x1fm[:].rearrange("p (g c) -> p g c", c=TT)[
                    :, 4 * half:4 * half + 4, 128 * q:128 * (q + 1)]
                with ctx.task("vector", [t.token]) as t2:
                    t2.done(v.tensor_copy(dst, PT[:]))
                PTb.read(t2.token)
                x1subs.append(t2.token)
        x1fm_b.wrote(ctx.barrier_task("vector", x1subs))
        hb = PVh[0]
        with ctx.task("tensor", hb.wdeps() + [t_wld] + x1fm_b.rdeps()) as t:
            for g in range(NG):
                last = te.matmul(PV[0:64, 0:512], winw[g][:],
                                 x1fm[:, TT * g:TT * (g + 1)],
                                 start=(g == 0), stop=(g == 7))
            t.done(last)
        x1fm_b.read(t.token)
        hb.wrote(t.token)
        with ctx.task("vector", [t.token, t_bias] + s_sb_b.wdeps()) as t2:
            t2.done(v.tensor_scalar_add(s_sb[:], PV[0:64, 0:512],
                                        binb_col[:, 0:1]))
        hb.read(t2.token)
        s_sb_b.wrote(t2.token)
        for o in range(NG):
            hb = PVh[o % 2]
            with ctx.task("tensor", hb.wdeps() + [t_wld] + s_sb_b.rdeps()) as t:
                t.done(te.matmul(PV[:, 512 * (o % 2):512 * (o % 2) + 512],
                                 w1emb[0:64, 128 * o:128 * (o + 1)], s_sb[:],
                                 start=True, stop=True))
            s_sb_b.read(t.token)
            hb.wrote(t.token)
            with ctx.task("scalar", [t.token, t_bias] +
                          (h_sb_b.wdeps() if o == 0 else [])) as t2:
                t2.done(sc.activation(h_sb[:, TT * o:TT * (o + 1)],
                                      PV[:, 512 * (o % 2):512 * (o % 2) + 512],
                                      AF.Relu, bias=b1e_col[:, o:o + 1]))
            hb.read(t2.token)
            h_sb_b.wrote(t2.token)
        hb = PVh[0]
        with ctx.task("tensor", hb.wdeps() + [t_wld] + h_sb_b.rdeps()) as t:
            for g in range(NG):
                last = te.matmul(PV[0:64, 0:512], w2emb[g][:],
                                 h_sb[:, TT * g:TT * (g + 1)],
                                 start=(g == 0), stop=(g == 7))
            t.done(last)
        h_sb_b.read(t.token)
        hb.wrote(t.token)
        with ctx.task("vector", [t.token, t_bias] + g_sb_b.wdeps()) as t2:
            t2.done(v.tensor_scalar_add(g_sb[:], PV[0:64, 0:512],
                                        b2b_col[:, 0:1]))
        hb.read(t2.token)
        g_sb_b.wrote(t2.token)
        for q in range(NSUB):
            for half in range(2):
                hb = PVh[half]
                with ctx.task("tensor", hb.wdeps() + [t_wld] +
                              g_sb_b.rdeps()) as t:
                    te.matmul(PV[:, 512 * half:512 * half + 512],
                              g_sb[:, 128 * q:128 * (q + 1)],
                              woutw[0:64, 512 * half:512 * half + 512],
                              start=True, stop=False)
                    last = te.matmul(PV[:, 512 * half:512 * half + 512],
                                     ones64,
                                     boutrow[:, 512 * half:512 * half + 512],
                                     start=False, stop=True)
                    t.done(last)
                g_sb_b.read(t.token)
                hb.wrote(t.token)
                with ctx.task("vector", [t.token] + x1_b[q].rdeps() +
                              x1_b[q].wdeps()) as t2:
                    t2.done(v.tensor_add(
                        x1[q][:, 512 * half:512 * (half + 1)],
                        x1[q][:, 512 * half:512 * (half + 1)],
                        PV[:, 512 * half:512 * half + 512]))
                hb.read(t2.token)
                x1_b[q].wrote(t2.token)
        t_ln2 = emit_ln_attn(ctx, nc, [x1_b[q].writer for q in range(NSUB)],
                        [x1[q][:] for q in range(NSUB)], st2,
                        [x_tm[q][:] for q in range(NSUB)], epst[:, 0:1],
                        extra_apply_deps=[t_eps] +
                        [tk for q in range(NSUB) for tk in x_tm_b[q].wdeps()])
        for q in range(NSUB):
            x1_b[q].read(t_ln2)
            x_tm_b[q].wrote(t_ln2)
            tst = ctx.dma("qact", y0_d[tok0 + 128 * q:tok0 + 128 * (q + 1), :],
                          x_tm[q][:], deps=[t_ln2])
            x_tm_b[q].read(tst)
            out_tasks.append(tst)

    with ctx.btask("sync", out_tasks) as t:
        ctx.wait_all_dmas("sync")
    return nc


NLEV_MT = 9
EPS = 1e-5
R75 = 1.0 / 7.5


def interleave(groups_ops):
    n = max(len(x) for x in groups_ops)
    last = None
    for i in range(n):
        for ops in groups_ops:
            if i < len(ops):
                last = ops[i]()
    return last


def bc(ap32):
    return ap32.unsqueeze(-1).broadcast_to([ap32.shape[0], ap32.shape[1], 32])


def emit_quant_small(ctx, nc, deps, pairs):
    v = nc.vector

    def filler():
        if len(pairs) < 3:
            for _ in range(3 - len(pairs)):
                v.memset(pairs[0][1][:, 160:192], 0.0)
    with ctx.task("vector", deps) as t:
        for x, s in pairs:
            v.tensor_reduce(s[:, 0:32].rearrange("p (b one) -> p b one", one=1),
                            x.rearrange("p (b k) -> p b k", k=32),
                            axis=AX.X, op=ALU.max, apply_absolute_value=True)
        filler()
        for x, s in pairs:
            v.tensor_scalar(s[:, 32:64], s[:, 0:32], R75, 1e-30,
                            ALU.mult, ALU.max)
        filler()
        for x, s in pairs:
            v.reciprocal(s[:, 64:96], s[:, 32:64])
        for x, s in pairs:
            v.tensor_scalar_mul(s[:, 96:128], s[:, 32:64], 2.0)
        for x, s in pairs:
            v.tensor_scalar_mul(s[:, 128:160], s[:, 32:64], 4.0)
        filler()
        for x, s in pairs:
            last = v.tensor_scalar_mul(s[:, 64:96], s[:, 64:96], 2.0)
        t.done(last)
    return t.token


def emit_quant_wide(ctx, nc, deps, x, s, B, xd_out, xbuf=None):
    v = nc.vector
    x3 = x.rearrange("p (b k) -> p b k", k=32)
    b1 = B["b1"][:].rearrange("p (b k) -> p b k", k=32)
    b2 = B["b2"][:].rearrange("p (b k) -> p b k", k=32)
    b3 = B["b3"][:].rearrange("p (b k) -> p b k", k=32)
    b5 = B["b5"][:].rearrange("p (b k) -> p b k", k=32)
    b6 = B["b6"][:].rearrange("p (b k) -> p b k", k=32)
    with ctx.task("scalar", deps) as t:
        t.done(nc.scalar.activation(B["b5"][:], x, AF.Abs))
    t_abs = t.token
    if xbuf is not None:
        xbuf.read(t_abs)
    with ctx.task("vector", deps + [t_abs]) as t:
        v.tensor_tensor(b1, b5, bc(s[:, 96:128]), ALU.is_lt)
        v.tensor_tensor(b2, b5, bc(s[:, 128:160]), ALU.is_lt)
        v.tensor_tensor(b3, x3, bc(s[:, 64:96]), ALU.mult)
        v.scalar_tensor_tensor(B["b3"][:], B["b1"][:], 1.0, B["b3"][:],
                               ALU.add, ALU.mult)
        v.scalar_tensor_tensor(B["b3"][:], B["b2"][:], 1.0, B["b3"][:],
                               ALU.add, ALU.mult)
        v.tensor_scalar(B["b6"][:], B["b2"][:], -0.25, 0.5, ALU.mult, ALU.add)
        v.scalar_tensor_tensor(B["b6"][:], B["b1"][:], -0.125, B["b6"][:],
                               ALU.mult, ALU.add)
        v.tensor_single_scalar(B["b3"][:], B["b3"][:], -0.5, ALU.add)
        v.tensor_copy(B["bi"][:], B["b3"][:])
        v.tensor_copy(B["b5"][:], B["bi"][:])
        v.tensor_tensor(B["b1"][:], B["b5"][:], B["b3"][:], ALU.is_lt)
        v.tensor_add(B["b2"][:], B["b5"][:], B["b1"][:])
        v.tensor_tensor(b6, b6, bc(s[:, 32:64]), ALU.mult)
        last = v.tensor_mul(xd_out, B["b2"][:], B["b6"][:])
        t.done(last)
    return t.token


def emit_ln_fin(ctx, nc, deps, xin, st, xout, eps_ap, extra_apply_deps=()):
    v = nc.vector
    sc = nc.scalar
    with ctx.task("vector", deps) as t:
        for x, s in zip(xin, st):
            v.bn_stats(s[:, 0:6], x[:, 0:512])
        for x, s in zip(xin, st):
            v.bn_stats(s[:, 6:12], x[:, 512:1024])
        for x, s in zip(xin, st):
            last = v.bn_aggr(s[:, 12:14], s[:, 0:12])
        t.done(last)
    t_stats = t.token
    with ctx.task("scalar", [t_stats]) as t:
        for s in st:
            sc.activation(s[:, 14:15], s[:, 13:14], AF.Ln, bias=eps_ap)
        for s in st:
            last = sc.activation(s[:, 14:15], s[:, 14:15], AF.Exp, scale=-0.5)
        t.done(last)
    t_rstd = t.token
    with ctx.task("vector", [t_rstd]) as t:
        for s in st:
            last = v.scalar_tensor_tensor(s[:, 15:16], s[:, 12:13], -1.0,
                                          s[:, 14:15], ALU.mult, ALU.mult)
        t.done(last)
    t_nmr = t.token
    with ctx.task("vector", [t_nmr] + list(extra_apply_deps)) as t:
        for x, s, xo in zip(xin, st, xout):
            last = v.tensor_scalar(xo, x, s[:, 14:15], s[:, 15:16],
                                   ALU.mult, ALU.add)
        t.done(last)
    return t.token


def load_vec_cols_f(ctx, nc, es, name, src1d, n):
    cols = n // 128
    out = es.enter_context(nc.sbuf_tensor(name, [128, cols], F32))
    tasks = []
    for o in range(cols):
        tasks.append(ctx.dma(
            "qsp", out[:, o:o + 1],
            src1d[128 * o:128 * (o + 1)].rearrange("(p one) -> p one", one=1)))
    return out, tasks


def build_finish(nc, es):
    ctx = Ctx(nc, es)
    z1_d = nc.dram_tensor("z1", [CH, TOK], F32, kind="ExternalInput").ap()
    clre_d = nc.dram_tensor("clre", [CH, NCH], F32, kind="ExternalInput").ap()
    clim_d = nc.dram_tensor("clim", [CH, NCH], F32, kind="ExternalInput").ap()
    efp_d = nc.dram_tensor("efp", [3, CH, 2], F32, kind="ExternalInput").ap()
    sel_d = nc.dram_tensor("sel", [128, 4], F32, kind="ExternalInput").ap()
    phz_d = nc.dram_tensor("phazor", [CH, 2], F32, kind="ExternalInput").ap()
    pin_d = nc.dram_tensor("phazor_init", [CH, 2], F32, kind="ExternalInput").ap()
    lcre_d = nc.dram_tensor("lcre", [CH], F32, kind="ExternalInput").ap()
    lcim_d = nc.dram_tensor("lcim", [CH], F32, kind="ExternalInput").ap()
    w_d = nc.dram_tensor("w", [CH, CH], F32, kind="ExternalInput").ap()
    bias_d = nc.dram_tensor("bias", [CH], F32, kind="ExternalInput").ap()
    out_d = nc.dram_tensor("out", [TOK, CH], F32, kind="ExternalOutput").ap()
    dbg_d = nc.dram_tensor("dbg", [7, 128, CH], F32, kind="ExternalOutput").ap()

    v = nc.vector
    sc = nc.scalar
    te = nc.tensor

    G, t_ph = build_ph_consts(ctx, nc, es, phz_d, pin_d, NLEV_MT)

    ident = es.enter_context(nc.sbuf_tensor("ident", [128, 128], F32))
    identb = es.enter_context(nc.sbuf_tensor("identb", [128, 128], BF16))
    ii = es.enter_context(nc.sbuf_tensor("ii", [128, 128], I32))
    ci = es.enter_context(nc.sbuf_tensor("ci", [128, 128], I32))
    with ctx.task("gpsimd", []) as t:
        nc.gpsimd.iota(ci[:], pattern=[[1, 128]], base=0, channel_multiplier=0)
        t.done(nc.gpsimd.iota(ii[:], pattern=[[0, 128]], base=0,
                              channel_multiplier=1))
    with ctx.task("vector", [t.token]) as t:
        v.tensor_tensor(ii[:], ci[:], ii[:], ALU.is_equal)
        v.tensor_copy(ident[:], ii[:])
        t.done(v.tensor_copy(identb[:], ii[:]))
    t_ident = t.token

    epst = es.enter_context(nc.sbuf_tensor("epst", [128, 1], F32))
    with ctx.task("vector", []) as t:
        t.done(v.memset(epst[:], EPS))
    t_eps = t.token

    pst = [es.enter_context(nc.psum_tensor(f"pst{i}", [128, 512], F32))
           for i in range(2)]
    pst_b = [Buf(), Buf()]
    psb = [es.enter_context(nc.psum_tensor(f"psb{i}", [128, 1024], BF16))
           for i in range(2)]
    psb_b = [Buf(), Buf()]
    pff = [es.enter_context(nc.psum_tensor(f"pff{i}", [128, 512], F32))
           for i in range(2)]
    pff_b = [Buf(), Buf()]
    pT = es.enter_context(nc.psum_tensor("pT", [128, 1024], F32))
    pT_b = Buf()

    small_tasks = [t_ph]
    for g in range(NG):
        gg = G[g]
        lc = es.enter_context(nc.sbuf_tensor(f"lc{g}", [128, 2], F32))
        ef = es.enter_context(nc.sbuf_tensor(f"ef{g}", [128, 6], F32))
        cw = es.enter_context(nc.sbuf_tensor(f"cwz{g}", [128, 12], F32))
        gg["lc"], gg["ef"], gg["cw"] = lc, ef, cw
        small_tasks.append(ctx.dma(
            "qsp", lc[:, 0:1],
            lcre_d[128 * g:128 * (g + 1)].rearrange("(p one) -> p one", one=1)))
        small_tasks.append(ctx.dma(
            "qsp", lc[:, 1:2],
            lcim_d[128 * g:128 * (g + 1)].rearrange("(p one) -> p one", one=1)))
        for k in range(3):
            small_tasks.append(ctx.dma("qsp", ef[:, 2 * k:2 * k + 2],
                                       efp_d[k, 128 * g:128 * (g + 1), :]))
    selt = es.enter_context(nc.sbuf_tensor("selt", [128, 4], F32))
    small_tasks.append(ctx.dma("qsp", selt[:], sel_d[:]))

    bias_col, t_bias_l = load_vec_cols_f(ctx, nc, es, "biascol", bias_d[:], CH)
    t_bias = ctx.barrier_task("sync", t_bias_l)

    def cw_ops(g):
        gg = G[g]
        cw = gg["cw"]
        mt = gg["mt"]
        c1r = mt[:, 2 * NLEV_MT:2 * NLEV_MT + 1]
        c1i = mt[:, 2 * NLEV_MT + 1:2 * NLEV_MT + 2]
        ef, lc = gg["ef"], gg["lc"]
        pir, pii = gg["pire"], gg["piim"]
        A = []

        def cmul_into(dst_r, dst_c, xr, xi, yr, yi, tr0, tr1, tr2, tr3,
                      addr=None, addi=None):
            A.append(lambda: v.tensor_mul(tr0, xr, yr))
            A.append(lambda: v.tensor_mul(tr1, xi, yi))
            A.append(lambda: v.tensor_mul(tr2, xr, yi))
            A.append(lambda: v.tensor_mul(tr3, xi, yr))
            if addr is None:
                A.append(lambda: v.tensor_sub(dst_r, tr0, tr1))
                A.append(lambda: v.tensor_add(dst_c, tr2, tr3))
            else:
                A.append(lambda: v.tensor_sub(tr0, tr0, tr1))
                A.append(lambda: v.tensor_add(tr2, tr2, tr3))
                A.append(lambda: v.tensor_add(dst_r, tr0, addr))
                A.append(lambda: v.tensor_add(dst_c, tr2, addi))
        t4, t5, t6, t7 = cw[:, 4:5], cw[:, 5:6], cw[:, 6:7], cw[:, 7:8]
        accr, acci = cw[:, 0:1], cw[:, 1:2]
        A.append(lambda: v.tensor_copy(accr, ef[:, 4:5]))
        A.append(lambda: v.tensor_copy(acci, ef[:, 5:6]))
        cmul_into(accr, acci, accr, acci, c1r, c1i, t4, t5, t6, t7,
                  ef[:, 2:3], ef[:, 3:4])
        cmul_into(accr, acci, accr, acci, c1r, c1i, t4, t5, t6, t7,
                  ef[:, 0:1], ef[:, 1:2])
        cmul_into(accr, acci, accr, acci, pir, pii, t4, t5, t6, t7)
        phr, phi = cw[:, 2:3], cw[:, 3:4]
        A.append(lambda: v.tensor_mul(t4, selt[:, 1:2], c1r))
        A.append(lambda: v.tensor_mul(t5, selt[:, 1:2], c1i))
        A.append(lambda: v.tensor_add(phr, selt[:, 0:1], t4))
        A.append(lambda: v.tensor_copy(phi, t5))
        cnr, cni = cw[:, 8:9], cw[:, 9:10]
        cmul_into(cnr, cni, c1r, c1i, c1r, c1i, t4, t5, t6, t7)
        A.append(lambda: v.tensor_mul(t4, selt[:, 2:3], cnr))
        A.append(lambda: v.tensor_mul(t5, selt[:, 2:3], cni))
        A.append(lambda: v.tensor_add(phr, phr, t4))
        A.append(lambda: v.tensor_add(phi, phi, t5))
        cmul_into(cnr, cni, cnr, cni, c1r, c1i, t4, t5, t6, t7)
        A.append(lambda: v.tensor_mul(t4, selt[:, 3:4], cnr))
        A.append(lambda: v.tensor_mul(t5, selt[:, 3:4], cni))
        A.append(lambda: v.tensor_add(phr, phr, t4))
        A.append(lambda: v.tensor_add(phi, phi, t5))
        cmul_into(t6, t7, lc[:, 0:1], lc[:, 1:2], phr, phi,
                  cw[:, 10:11], cw[:, 11:12], t4, t5)
        A.append(lambda: v.tensor_add(accr, accr, t6))
        A.append(lambda: v.tensor_add(acci, acci, t7))
        return A

    with ctx.task("vector", small_tasks + [t_ph]) as t:
        t.done(interleave([cw_ops(g) for g in range(NG)]))
    t_cw = t.token

    dre = [es.enter_context(nc.sbuf_tensor(f"dre{g}", [128, NCH], F32))
           for g in range(NG)]
    dim = [es.enter_context(nc.sbuf_tensor(f"dim{g}", [128, NCH], F32))
           for g in range(NG)]
    cwpr = es.enter_context(nc.sbuf_tensor("cwpr", [128, NCH], F32))
    cwpi = es.enter_context(nc.sbuf_tensor("cwpi", [128, NCH], F32))
    qs1 = [es.enter_context(nc.sbuf_tensor(f"qs1{g}", [128, 256], F32))
           for g in range(NG)]
    qs2 = [es.enter_context(nc.sbuf_tensor(f"qs2{g}", [128, 256], F32))
           for g in range(NG)]
    cl_tasks = []
    for g in range(NG):
        cl_tasks.append(ctx.dma("qsp", dre[g][:],
                                clre_d[128 * g:128 * (g + 1), :]))
        cl_tasks.append(ctx.dma("qsp", dim[g][:],
                                clim_d[128 * g:128 * (g + 1), :]))
    with ctx.task("vector", cl_tasks + [t_cw]) as t:
        for half in range(2):
            h0 = half * 256
            for g in range(NG):
                pir, pii = G[g]["pire"], G[g]["piim"]
                v.tensor_scalar_mul(qs1[g][:, 0:256], dim[g][:, h0:h0 + 256],
                                    pii)
                v.tensor_scalar_mul(qs2[g][:, 0:256], dre[g][:, h0:h0 + 256],
                                    pii)
                v.scalar_tensor_tensor(dre[g][:, h0:h0 + 256],
                                       dre[g][:, h0:h0 + 256], pir,
                                       qs1[g][:, 0:256], ALU.mult,
                                       ALU.subtract)
                v.scalar_tensor_tensor(dim[g][:, h0:h0 + 256],
                                       dim[g][:, h0:h0 + 256], pir,
                                       qs2[g][:, 0:256], ALU.mult, ALU.add)
        for g in range(NG):
            ggm = G[g]["mt"]
            v.tensor_copy(cwpr[:, 0:1], G[g]["cw"][:, 0:1])
            v.tensor_copy(cwpi[:, 0:1], G[g]["cw"][:, 1:2])
            for l in range(9):
                n = 1 << l
                ma = ggm[:, 2 * l:2 * l + 1]
                mb = ggm[:, 2 * l + 1:2 * l + 2]
                v.tensor_scalar_mul(qs1[g][:, 0:n], cwpi[:, 0:n], mb)
                v.tensor_scalar_mul(qs2[g][:, 0:n], cwpr[:, 0:n], mb)
                v.scalar_tensor_tensor(cwpr[:, n:2 * n], cwpr[:, 0:n], ma,
                                       qs1[g][:, 0:n], ALU.mult, ALU.subtract)
                v.scalar_tensor_tensor(cwpi[:, n:2 * n], cwpi[:, 0:n], ma,
                                       qs2[g][:, 0:n], ALU.mult, ALU.add)
            v.tensor_add(dre[g][:], dre[g][:], cwpr[:])
            last = v.tensor_add(dim[g][:], dim[g][:], cwpi[:])
        t.done(last)
    t_dtot = t.token

    wstage = es.enter_context(nc.sbuf_tensor("wstage", [128, CH], F32))
    wstage_b = Buf()
    wq = es.enter_context(nc.sbuf_tensor("wq", [128, CH], BF16))
    wq_b = Buf()
    wsm = es.enter_context(nc.sbuf_tensor("wsm", [128, 192], F32))
    QB = dict(
        b1=es.enter_context(nc.sbuf_tensor("qb1", [128, 1024], F32)),
        b2=es.enter_context(nc.sbuf_tensor("qb2", [128, 1024], F32)),
        b3=es.enter_context(nc.sbuf_tensor("qb3", [128, 1024], F32)),
        b5=es.enter_context(nc.sbuf_tensor("qb5", [128, 1024], F32)),
        b6=es.enter_context(nc.sbuf_tensor("qb6", [128, 1024], F32)),
        bi=es.enter_context(nc.sbuf_tensor("qbi", [128, 1024], I32)),
    )
    QB_b = Buf()
    wdT = [es.enter_context(nc.sbuf_tensor(f"wdT{g}", [128, CH], BF16))
           for g in range(NG)]
    wdT_b = [Buf() for _ in range(NG)]
    for o in range(NG):
        tld = ctx.dma("qsp", wstage[:], w_d[128 * o:128 * (o + 1), :],
                      deps=wstage_b.wdeps())
        wstage_b.wrote(tld)
        tq1 = emit_quant_small(ctx, nc, [tld] + QB_b.wdeps(),
                               [(wstage[:], wsm)])
        tq2 = emit_quant_wide(ctx, nc, [tq1] + wq_b.wdeps(), wstage[:], wsm,
                              QB, wq[:])
        wstage_b.read(tq2)
        QB_b.wrote(tq2)
        wq_b.wrote(tq2)
        for g in range(NG):
            pb, pbb = psb[g % 2], psb_b[g % 2]
            with ctx.task("tensor", [tq2, t_ident] + pbb.wdeps()) as t:
                t.done(te.transpose(pb[:, 0:128], wq[:, 128 * g:128 * (g + 1)],
                                    identb[:]))
            pbb.wrote(t.token)
            wq_b.read(t.token)
            with ctx.task("vector", [t.token]) as t2:
                t2.done(v.tensor_copy(wdT[g][:, 128 * o:128 * (o + 1)],
                                      pb[:, 0:128]))
            pbb.read(t2.token)
            wdT_b[g].wrote(t2.token)

    zfm = [[es.enter_context(nc.sbuf_tensor(f"zfm{i}_{g}", [128, TT], F32))
            for g in range(NG)] for i in range(1)]
    zfm_b = [[Buf() for _ in range(NG)] for i in range(1)]
    ztm = [es.enter_context(nc.sbuf_tensor(f"ztm{q}", [128, CH], F32))
           for q in range(NSUB)]
    ztm_b = [Buf() for _ in range(NSUB)]
    xl1 = [es.enter_context(nc.sbuf_tensor(f"xl1{q}", [128, CH], F32))
           for q in range(NSUB)]
    xl1_b = [Buf() for _ in range(NSUB)]
    x2 = ztm
    x2_b = ztm_b
    outq = [es.enter_context(nc.sbuf_tensor(f"outq{q}", [128, CH], F32))
            for q in range(NSUB)]
    outq_b = [Buf() for _ in range(NSUB)]
    st3 = [es.enter_context(nc.sbuf_tensor(f"st3{q}", [128, 16], F32))
           for q in range(NSUB)]
    st4 = [es.enter_context(nc.sbuf_tensor(f"st4{q}", [128, 16], F32))
           for q in range(NSUB)]
    xsm = [es.enter_context(nc.sbuf_tensor(f"xsm{q}", [128, 192], F32))
           for q in range(NSUB)]
    xd = [es.enter_context(nc.sbuf_tensor(f"xd{q}", [128, CH], BF16))
          for q in range(NSUB)]
    xd_b = [Buf() for _ in range(NSUB)]
    xdfm = [es.enter_context(nc.sbuf_tensor(f"xdfm{g}", [128, TT], BF16))
            for g in range(NG)]
    xdfm_b = [Buf() for _ in range(NG)]
    ffo = [es.enter_context(nc.sbuf_tensor(f"ffo{o}", [128, TT], F32))
           for o in range(NG)]
    ffo_b = [Buf() for _ in range(NG)]

    out_tasks = []
    for ti in range(NT):
        tok0 = ti * TT
        c0 = tok0 // C
        zi = 0
        zlds = []
        for g in range(NG):
            tld = ctx.dma("qsp", zfm[zi][g][:],
                          z1_d[128 * g:128 * (g + 1), tok0:tok0 + TT],
                          deps=zfm_b[zi][g].wdeps())
            zfm_b[zi][g].wrote(tld)
            zlds.append(tld)
        for g in range(NG):
            gg = G[g]
            zv = zfm[zi][g][:].rearrange("p (c j) -> p j c", j=C)
            dr = dre[g][:, c0:c0 + TT // C]
            dmm = dim[g][:, c0:c0 + TT // C]
            with ctx.task("vector", [zlds[g], t_dtot] +
                          zfm_b[zi][g].rdeps()) as t:
                for j in range(C):
                    aj = gg["p0re"][:, j + 1:j + 2]
                    bj = gg["p0im"][:, j + 1:j + 2]
                    zj = zv[:, j, :]
                    v.scalar_tensor_tensor(qs1[g][:, 0:TT // C], dmm, bj, zj,
                                           ALU.mult, ALU.subtract)
                    last = v.scalar_tensor_tensor(zj, dr, aj,
                                                  qs1[g][:, 0:TT // C],
                                                  ALU.mult, ALU.subtract)
                t.done(last)
            zfm_b[zi][g].wrote(t.token)

        for q in range(NSUB):
            pp, ppb = pst[q % 2], pst_b[q % 2]
            tts = []
            for g in range(NG):
                with ctx.task("tensor", zfm_b[zi][g].rdeps() + [t_ident] +
                              ppb.wdeps()) as t:
                    t.done(te.transpose(
                        pp[:, 0:128],
                        zfm[zi][g][:, 128 * q:128 * (q + 1)], ident[:]))
                zfm_b[zi][g].read(t.token)
                ppb.wrote(t.token)
                with ctx.task("vector", [t.token] +
                              (ztm_b[q].wdeps() if g == 0 else [])) as t2:
                    t2.done(v.tensor_copy(ztm[q][:, 128 * g:128 * (g + 1)],
                                          pp[:, 0:128]))
                ppb.read(t2.token)
                tts.append(t2.token)
            ztm_b[q].wrote(ctx.barrier_task("vector", tts))
            if ti == 0 and q == 0:
                out_tasks.append(ctx.dma("qact", dbg_d[0], ztm[0][:],
                                         deps=ztm_b[0].rdeps()))
                ztm_b[0].read(out_tasks[-1])

        t_ln3 = emit_ln_fin(ctx, nc,
                        [tk for q in range(NSUB) for tk in ztm_b[q].rdeps()],
                        [ztm[q][:] for q in range(NSUB)],
                        st3,
                        [xl1[q][:] for q in range(NSUB)],
                        epst[:, 0:1],
                        extra_apply_deps=[tk for q in range(NSUB)
                                          for tk in xl1_b[q].wdeps()] +
                                         [t_eps])
        for q in range(NSUB):
            ztm_b[q].read(t_ln3)
            xl1_b[q].wrote(t_ln3)
        if ti == 0:
            out_tasks.append(ctx.dma("qact", dbg_d[1], xl1[0][:],
                                     deps=xl1_b[0].rdeps()))
            xl1_b[0].read(out_tasks[-1])

        tqs = emit_quant_small(ctx, nc, [t_ln3],
                               [(xl1[q][:], xsm[q]) for q in range(NSUB)])
        for q in range(NSUB):
            xl1_b[q].read(tqs)
            tqw = emit_quant_wide(ctx, nc, [tqs] + QB_b.wdeps() +
                                  xd_b[q].wdeps(),
                                  xl1[q][:], xsm[q], QB, xd[q][:],
                                  xbuf=xl1_b[q])
            QB_b.wrote(tqw)
            xl1_b[q].read(tqw)
            xd_b[q].wrote(tqw)
            if ti == 0 and q == 0:
                with ctx.task("vector", xd_b[0].rdeps() + QB_b.wdeps()) as tdx:
                    tdx.done(v.tensor_copy(QB["b1"][:], xd[0][:]))
                QB_b.wrote(tdx.token)
                xd_b[0].read(tdx.token)
                out_tasks.append(ctx.dma("qact", dbg_d[2], QB["b1"][:],
                                         deps=[tdx.token]))
                QB_b.read(out_tasks[-1])

        for q in range(NSUB):
            for g in range(NG):
                pb, pbb = psb[g % 2], psb_b[g % 2]
                with ctx.task("tensor", xd_b[q].rdeps() + [t_ident] +
                              pbb.wdeps()) as t:
                    t.done(te.transpose(pb[:, 0:128],
                                        xd[q][:, 128 * g:128 * (g + 1)],
                                        identb[:]))
                xd_b[q].read(t.token)
                pbb.wrote(t.token)
                with ctx.task("vector", [t.token] +
                              (xdfm_b[g].wdeps() if q == 0 else [])) as t2:
                    t2.done(v.tensor_copy(xdfm[g][:, 128 * q:128 * (q + 1)],
                                          pb[:, 0:128]))
                pbb.read(t2.token)
                xdfm_b[g].wrote(t2.token)

        for o in range(NG):
            pf, pfb = pff[o % 2], pff_b[o % 2]
            with ctx.task("tensor",
                          [wdT_b[g].writer for g in range(NG)] +
                          [xdfm_b[g].writer for g in range(NG)] +
                          pfb.wdeps()) as t:
                for g in range(NG):
                    last = te.matmul(pf[:], wdT[g][:, 128 * o:128 * (o + 1)],
                                     xdfm[g][:], start=(g == 0), stop=(g == 7))
                t.done(last)
            for g in range(NG):
                xdfm_b[g].read(t.token)
            pfb.wrote(t.token)
            with ctx.task("vector", [t.token, t_bias] + ffo_b[o].wdeps()) as t2:
                t2.done(v.tensor_scalar_add(ffo[o][:], pf[:],
                                            bias_col[:, o:o + 1]))
            pfb.read(t2.token)
            ffo_b[o].wrote(t2.token)

        if ti == 0:
            with ctx.task("vector", ffo_b[0].rdeps() + QB_b.wdeps()) as tdf:
                tdf.done(v.tensor_copy(QB["b1"][:, 0:TT], ffo[0][:]))
            QB_b.wrote(tdf.token)
            ffo_b[0].read(tdf.token)
            out_tasks.append(ctx.dma("qact", dbg_d[4, :, 0:TT],
                                     QB["b1"][:, 0:TT], deps=[tdf.token]))
            QB_b.read(out_tasks[-1])
            with ctx.task("vector", [wdT_b[0].writer] + QB_b.wdeps()) as tdw:
                tdw.done(v.tensor_copy(QB["b1"][:], wdT[0][:]))
            QB_b.wrote(tdw.token)
            out_tasks.append(ctx.dma("qact", dbg_d[5], QB["b1"][:],
                                     deps=[tdw.token]))
            QB_b.read(out_tasks[-1])
            with ctx.task("vector", [xdfm_b[0].writer] + QB_b.wdeps()) as tdx2:
                tdx2.done(v.tensor_copy(QB["b1"][:, 0:TT], xdfm[0][:]))
            QB_b.wrote(tdx2.token)
            xdfm_b[0].read(tdx2.token)
            out_tasks.append(ctx.dma("qact", dbg_d[6, :, 0:TT],
                                     QB["b1"][:, 0:TT], deps=[tdx2.token]))
            QB_b.read(out_tasks[-1])
        for q in range(NSUB):
            ftts = []
            for o in range(NG):
                with ctx.task("tensor", ffo_b[o].rdeps() + [t_ident] +
                              (pT_b.wdeps() if o == 0 else [])) as t:
                    t.done(te.transpose(pT[:, 128 * o:128 * (o + 1)],
                                        ffo[o][:, 128 * q:128 * (q + 1)],
                                        ident[:]))
                ffo_b[o].read(t.token)
                pT_b.wrote(t.token)
                ftts.append(t.token)
            with ctx.task("vector", ftts + xl1_b[q].rdeps() +
                          x2_b[q].wdeps()) as t:
                t.done(v.tensor_add(x2[q][:], xl1[q][:], pT[:]))
            pT_b.read(t.token)
            xl1_b[q].read(t.token)
            x2_b[q].wrote(t.token)
            if ti == 0 and q == 0:
                out_tasks.append(ctx.dma("qact", dbg_d[3], x2[0][:],
                                         deps=x2_b[0].rdeps()))
                x2_b[0].read(out_tasks[-1])

        t_ln4 = emit_ln_fin(ctx, nc,
                        [x2_b[q].writer for q in range(NSUB)],
                        [x2[q][:] for q in range(NSUB)],
                        st4,
                        [outq[q][:] for q in range(NSUB)],
                        epst[:, 0:1],
                        extra_apply_deps=[tk for q in range(NSUB)
                                          for tk in outq_b[q].wdeps()] +
                                         [t_eps])
        for q in range(NSUB):
            x2_b[q].read(t_ln4)
            outq_b[q].wrote(t_ln4)
            tst = ctx.dma("qact",
                          out_d[tok0 + 128 * q: tok0 + 128 * (q + 1), :],
                          outq[q][:], deps=[t_ln4])
            outq_b[q].read(tst)
            out_tasks.append(tst)

    with ctx.btask("sync", out_tasks) as t:
        ctx.wait_all_dmas("sync")
    return nc


NEG_H = -1e9


def _host_consts():
    maskab = np.zeros((2, 128, 256), np.float32)
    i_ = np.arange(128)[:, None]
    cA = np.arange(128)[None, :]
    maskab[0, :, 0:128] = np.where(cA >= i_, 0.0, NEG_H)
    maskab[0, :, 128:256] = np.where(cA < i_, 0.0, NEG_H)
    maskab[1, :, 0:128] = NEG_H
    maskab[1, :, 128:256] = maskab[0, :, 128:256]
    e2 = np.zeros((2, 128), np.float32)
    e2[0, 0:64] = 1.0
    e2[1, 64:128] = 1.0
    return maskab, e2


def kernel(**inputs):
    import contextlib
    from concourse.bass_utils import run_bass_kernel_spmd

    f = lambda k: np.ascontiguousarray(np.asarray(inputs[k], dtype=np.float32))
    x = f("L0_attn_x")
    B, S, D = x.shape
    TOKC = 4096
    cores = [(c // 4, c % 4) for c in range(8)]

    nc = bass.Bass("TRN2", target_bir_lowering=False)
    with contextlib.ExitStack() as es:
        build_pooled(nc, es)
    in_maps = [{"x": x[b, s * TOKC:(s + 1) * TOKC]} for b, s in cores]
    res = run_bass_kernel_spmd(nc, in_maps, core_ids=list(range(8)))
    pooled_full = [np.concatenate(
        [res.results[4 * b + s]["pooled"] for s in range(4)], axis=1)
        for b in range(B)]

    maskab, e2 = _host_consts()
    W1 = f("L0_ffn_W1")
    W2 = f("L0_ffn_W2")
    W1emb = np.zeros((64, 1024), np.float32)
    W2emb = np.zeros((1024, 64), np.float32)
    for m in range(64):
        W1emb[m, 16 * m:16 * (m + 1)] = W1[m]
        W2emb[16 * m:16 * (m + 1), m] = W2[m]
    gl = f("L0_attn_gl_w")
    gr = f("L0_attn_gr_w")
    gldiag = np.zeros((128, 2), np.float32)
    grdiag = np.zeros((128, 2), np.float32)
    gldiag[0:64, 0] = gl[:, 0]
    gldiag[64:128, 1] = gl[:, 0]
    grdiag[0:64, 0] = gr[:, 0]
    grdiag[64:128, 1] = gr[:, 0]
    nc = bass.Bass("TRN2", target_bir_lowering=False)
    with contextlib.ExitStack() as es:
        build_attn(nc, es)
    in_maps = []
    for b, s in cores:
        st = s * TOKC
        halo = (np.zeros((128, D), np.float32) if s == 0
                else x[b, st - 128:st])
        xh = np.concatenate([halo, x[b, st:st + TOKC]], 0)
        in_maps.append(dict(
            xh=xh, pooled=pooled_full[b],
            qkv_w=f("L0_attn_qkv_w"), qkv_b=f("L0_attn_qkv_b"),
            rq_w=f("L0_attn_rq_w"), rq_b=f("L0_attn_rq_b"),
            rkv_w=f("L0_attn_rkv_w"), rkv_b=f("L0_attn_rkv_b"),
            rout_w=f("L0_attn_rout_w"), rout_b=f("L0_attn_rout_b"),
            out_w=f("L0_attn_out_w"), out_b=f("L0_attn_out_b"),
            win=f("L0_ffn_Win"), binb=f("L0_ffn_bin_"),
            w1emb=W1emb, b1emb=f("L0_ffn_b1").reshape(-1),
            w2emb=W2emb, b2b=f("L0_ffn_b2"),
            wout=f("L0_ffn_Wout"), bout=f("L0_ffn_bout"),
            gldiag=gldiag, grdiag=grdiag, maskab=maskab, e2=e2,
            isfirst=np.full((128, 1), 1.0 if s == 0 else 0.0, np.float32),
            cid0=np.full((1, 1), 16.0 * s, np.float32)))
    res = run_bass_kernel_spmd(nc, in_maps, core_ids=list(range(8)))
    y0 = [res.results[c]["y0"] for c in range(8)]

    nc = bass.Bass("TRN2", target_bir_lowering=False)
    with contextlib.ExitStack() as es:
        build(nc, es)
    in_maps = [{"y0": y0[c], "phazor": f("L1_attn_phazor"),
                "phazor_init": f("L1_attn_phazor_init")} for c in range(8)]
    res = run_bass_kernel_spmd(nc, in_maps, core_ids=list(range(8)))
    z1 = [res.results[c]["z1"] for c in range(8)]
    cls = [(res.results[c]["clre"], res.results[c]["clim"]) for c in range(8)]
    efs = [res.results[c]["ef"] for c in range(8)]

    nc = bass.Bass("TRN2", target_bir_lowering=False)
    with contextlib.ExitStack() as es:
        build_finish(nc, es)
    in_maps = []
    for c, (b, s) in enumerate(cores):
        efp = np.zeros((3, 1024, 2), np.float32)
        for k in range(3):
            sp = s - 1 - k
            if sp >= 0:
                efp[k] = efs[4 * b + sp]
        sel = np.zeros((128, 4), np.float32)
        sel[:, s] = 1.0
        in_maps.append(dict(
            z1=z1[c], clre=cls[c][0], clim=cls[c][1], efp=efp, sel=sel,
            phazor=f("L1_attn_phazor"), phazor_init=f("L1_attn_phazor_init"),
            lcre=f("L1_attn_last_conv_re"), lcim=f("L1_attn_last_conv_im"),
            w=f("L1_ffn_weight"), bias=f("L1_ffn_bias")))
    res = run_bass_kernel_spmd(nc, in_maps, core_ids=list(range(8)))
    out = np.empty((B, S, D), np.float32)
    for c, (b, s) in enumerate(cores):
        out[b, s * TOKC:(s + 1) * TOKC] = res.results[c]["out"]
    return out
